# version 04
# baseline (speedup 1.0000x reference)
import numpy as np
import ml_dtypes
import concourse.bass as bass
import concourse.mybir as mybir
from concourse.bass_utils import run_bass_kernel_spmd
from contextlib import ExitStack
from collections import defaultdict

F32 = mybir.dt.float32
BF16 = mybir.dt.bfloat16
FP16 = mybir.dt.float16
AF = mybir.ActivationFunctionType
ALU = mybir.AluOpType
AX = mybir.AxisListType.X

S, D, NCH = 2048, 1024, 8
H, KVH, HD, LAT, WIN = 16, 4, 64, 256, 256
E, HF = 8, 2048
NEG = -30000.0

OFF_XN, OFF_A, OFF_B, OFF_C, OFF_W, OFF_K, OFF_T = (
    0, 32768, 65536, 98304, 131072, 163840, 196608)
ARENA = 208896
PREP_SUB = 9
TEST_WLOADS = True
TEST_ATTN = True
ENGINES = ("sync", "tensor", "vector", "scalar", "gpsimd")


def _host_constants():
    c = {}
    c["ident"] = np.eye(128, dtype=ml_dtypes.bfloat16)
    kk = np.arange(128)[:, None]
    jj = np.arange(128)[None, :]
    c["mc"] = np.where(jj >= kk, 0.0, NEG).astype(np.float32)
    jj2 = np.arange(384)[None, :]
    c["ml"] = np.where((jj2 >= kk) & (jj2 < kk + WIN), 0.0, NEG).astype(np.float32)
    for name, base in (("g", 1_000_000.0), ("l", 10_000.0)):
        inv = 1.0 / base ** (np.arange(0, HD, 2, dtype=np.float64) / HD)
        fr = np.arange(S, dtype=np.float64)[:, None] * inv[None, :]
        cos = np.tile(np.cos(fr), (1, 16))
        sin = np.tile(np.sin(fr), (1, 16))
        c[f"rep_{name}"] = np.concatenate([cos, sin], axis=1).astype(np.float16)
    ind = np.zeros((8, 8 * 128), dtype=np.float32)
    for e in range(8):
        ind[e, 128 * e:128 * e + 128] = 1.0
    c["ind8"] = ind
    c["ones128_bf"] = np.ones((128, 1), dtype=ml_dtypes.bfloat16)
    c["ones1_f32"] = np.ones((1, 128), dtype=np.float32)
    return c


def _col_segs(j0, j1):
    segs = []
    for b0 in (0, 512):
        lo, hi = max(j0, b0), min(j1, b0 + 512)
        if lo < hi:
            segs.append((lo, hi))
    return segs


def build_nc(taps=(), stage=99):
    nc = bass.Bass("TRN2", target_bir_lowering=False, debug=False,
                   detect_race_conditions=False)
    cst = _host_constants()

    dx = nc.dram_tensor("x", [S, D], F32, kind="ExternalInput").ap()
    dn1 = nc.dram_tensor("n1w", [D, 1], F32, kind="ExternalInput").ap()
    dn2 = nc.dram_tensor("n2w", [D, 1], F32, kind="ExternalInput").ap()
    dgwq = nc.dram_tensor("gwq", [D, D], F32, kind="ExternalInput").ap()
    dgkvd = nc.dram_tensor("gkvd", [D, LAT], F32, kind="ExternalInput").ap()
    dgkup = nc.dram_tensor("gkup", [LAT, LAT], F32, kind="ExternalInput").ap()
    dgvup = nc.dram_tensor("gvup", [LAT, LAT], F32, kind="ExternalInput").ap()
    dgwo = nc.dram_tensor("gwo", [D, D], F32, kind="ExternalInput").ap()
    dlwq = nc.dram_tensor("lwq", [D, D], F32, kind="ExternalInput").ap()
    dlwk = nc.dram_tensor("lwk", [D, LAT], F32, kind="ExternalInput").ap()
    dlwv = nc.dram_tensor("lwv", [D, LAT], F32, kind="ExternalInput").ap()
    dlwo = nc.dram_tensor("lwo", [D, D], F32, kind="ExternalInput").ap()
    drwT = nc.dram_tensor("rwT", [D, E], F32, kind="ExternalInput").ap()
    dw1 = nc.dram_tensor("w1", [E, D, HF], F32, kind="ExternalInput").ap()
    dw2 = nc.dram_tensor("w2", [E, HF, D], F32, kind="ExternalInput").ap()
    dout = nc.dram_tensor("out", [S, D], F32, kind="ExternalOutput").ap()
    dw1b = nc.dram_tensor("w1b", [E, D, HF], BF16, kind="Internal").ap()
    dw2b = nc.dram_tensor("w2b", [E, HF, D], BF16, kind="Internal").ap()

    tapt = {}
    tapshape = dict(xn=([128, 16384], BF16), a1=([128, 2048], BF16),
                    qg=([128, 16384], BF16), kg0=([128, 2048], BF16),
                    kg1=([128, 2048], BF16), vg=([128, 4160], BF16),
                    gout=([128, 16384], BF16), h1=([128, 16384], BF16),
                    ql=([128, 16384], BF16), lout=([128, 16384], BF16),
                    s=([128, 16384], BF16), szm=([8, 2048], F32),
                    mkf=([8, 2048], BF16), rcpd=([128, 16], F32),
                    qps=([128, 1024], BF16), qro=([128, 1024], F32),
                    qn=([128, 1024], BF16), smallt=([128, 96], F32))
    for name in taps:
        sh, dt = tapshape[name]
        tapt[name] = nc.dram_tensor("tap_" + name, sh, dt,
                                    kind="ExternalOutput").ap()

    c_ident = nc.inline_tensor(np.asarray(cst["ident"]), "c_ident")
    c_mc = nc.inline_tensor(cst["mc"], "c_mc")
    c_ml = nc.inline_tensor(cst["ml"], "c_ml")
    c_repg = nc.inline_tensor(np.asarray(cst["rep_g"]), "c_repg")
    c_repl = nc.inline_tensor(np.asarray(cst["rep_l"]), "c_repl")
    c_ind8 = nc.inline_tensor(cst["ind8"], "c_ind8")
    c_o128 = nc.inline_tensor(np.asarray(cst["ones128_bf"]), "c_o128")
    c_o1 = nc.inline_tensor(cst["ones1_f32"], "c_o1")

    with ExitStack() as stk:
        en = stk.enter_context
        arena = nc.alloc_sbuf_tensor("arena", [128, ARENA], mybir.dt.uint8)
        abase = nc.lookup_mloc(arena).addr

        def at(name, off, shape, dtype):
            return nc.alloc_sbuf_tensor_at(name, shape, dtype, offset=abase + off)

        XN = at("XN", OFF_XN, [128, 16384], BF16)
        BA = at("BA", OFF_A, [128, 16384], BF16)
        BB = at("BB", OFF_B, [128, 16384], BF16)
        BC = at("BC", OFF_C, [128, 16384], BF16)
        BW = at("BW", OFF_W, [128, 16384], BF16)

        K0 = at("K0", OFF_K, [128, 2048], BF16)
        K1 = at("K1", OFF_K + 4096, [128, 2048], BF16)
        VA = at("VA", OFF_K + 8192, [128, 4160], BF16)
        ROPE = at("ROPE", OFF_K + 16512, [128, 2048], FP16)
        QSQ = at("QSQ", OFF_K + 20608, [128, 1024], BF16)
        QRO = at("QRO", OFF_K + 22656, [128, 1024], F32)
        QN = at("QN", OFF_K + 26752, [128, 1024], BF16)
        LATS = at("LATS", OFF_K + 28800, [128, 512], BF16)
        KSQ = at("KSQ", OFF_K + 29824, [128, 256], BF16)
        KRO = at("KRO", OFF_K + 30336, [128, 256], F32)
        KN = at("KN", OFF_K + 31360, [128, 256], BF16)
        PROBS = at("PROBS", OFF_K + 20608, [128, 2048], BF16)
        MSCR = at("MSCR", OFF_K + 24704, [128, 384], F32)
        RBCS = at("RBCS", OFF_K + 26240, [128, 512], F32)
        RDR = at("RDR", OFF_K + 28288, [1, 1024], F32)
        SRZ = at("SRZ", OFF_K, [8, 512], F32)
        SZM = at("SZM", OFF_K + 2048, [8, 2048], F32)
        MKF = at("MKF", OFF_K + 10240, [8, 2048], BF16)
        WBCS = at("WBCS", OFF_K + 14336, [128, 2048], F32)

        XST = at("XST", OFF_T, [128, 2048], BF16)
        ATILE = at("ATILE", OFF_T, [128, 2048], BF16)
        GSB = at("GSB", OFF_T, [128, 1024], F32)
        OUTB = at("OUTB", OFF_T, [128, 1024], F32)
        LGT = at("LGT", OFF_T, [128, 48], F32)
        MKT = at("MKT", OFF_T + 192, [128, 8], BF16)
        ZTM = at("ZTM", OFF_T + 224, [128, 8], F32)
        XSQ = at("XSQ", OFF_T + 4096, [128, 512], BF16)
        OUTA = at("OUTA", OFF_T + 4096, [128, 1024], F32)
        XNT = at("XNT", OFF_T + 5120, [128, 256], F32)
        ROWS = at("ROWS", OFF_T + 6144, [1, 768], F32)
        PRR = at("PRR", OFF_T + 6144, [1, 512], F32)
        SMALLT = at("SMALLT", OFF_T + 5120, [128, 96], F32)
        RSC1 = at("RSC1", OFF_T + 8192, [128, 512], F32)
        RSC2 = at("RSC2", OFF_T + 10240, [128, 512], F32)
        XRL = at("XRL", OFF_T + 8192, [128, 1024], F32)
        IND8 = at("IND8", OFF_K + 22528, [8, 1024], F32)
        HBF = at("HBF", OFF_K + 26624, [128, 1024], BF16)
        ASCR = at("ASCR", OFF_K + 28672, [128, 512], F32)

        N1W = nc.alloc_sbuf_tensor("N1W", [128, 8], F32)
        N2W = nc.alloc_sbuf_tensor("N2W", [128, 8], F32)
        RWT = nc.alloc_sbuf_tensor("RWT", [128, 64], BF16)
        RCPD = nc.alloc_sbuf_tensor("RCPD", [128, 16], F32)
        CC = nc.alloc_sbuf_tensor("CC", [128, 8], F32)
        CM05 = nc.alloc_sbuf_tensor("CM05", [128, 1], F32)
        IDENT = nc.alloc_sbuf_tensor("IDENT", [128, 128], BF16)
        MCT = nc.alloc_sbuf_tensor("MCT", [128, 128], F32)
        MLT = nc.alloc_sbuf_tensor("MLT", [128, 384], F32)
        O128 = nc.alloc_sbuf_tensor("O128", [128, 1], BF16)
        ZCOL = nc.alloc_sbuf_tensor("ZCOL", [128, 65], BF16)
        O1 = nc.alloc_sbuf_tensor("O1", [1, 128], F32)
        assert nc.sbuf_bytes_remaining >= 0, nc.sbuf_bytes_remaining

        PSA = en(nc.psum_tensor("PSA", [128, 1024], F32))
        PSB = en(nc.psum_tensor("PSB", [128, 1024], F32))
        PSC = en(nc.psum_tensor("PSC", [128, 1024], F32))
        PSD = en(nc.psum_tensor("PSD", [128, 512], F32))
        PST = en(nc.psum_tensor("PST", [128, 1024], BF16))

        prog = []
        counts = defaultdict(int)
        guards = {}

        def emit(engine, fn, waits=(), inc=None, ninc=1, reads=(), writes=()):
            wl = list(waits)
            for slot in writes:
                g = guards.get(slot)
                if g is not None:
                    wl.append(g)
            wl = [(s, v) for (s, v) in wl if v > 0]
            dd = {}
            for s, v in wl:
                dd[s] = max(dd.get(s, 0), v)
            prog.append((engine, fn, sorted(dd.items()), inc, ninc))
            if inc is not None:
                counts[inc] += ninc
                for slot in reads:
                    guards[slot] = (inc, counts[inc])

        def W(s, v=None):
            return (s, counts[s] if v is None else v)

        def fc(c, t0):
            return 2048 * c + t0

        def phase_guard(name):
            snap = [(s, c) for s, c in counts.items()
                    if c > 0 and not s.startswith("d_")]
            for eng in ("tensor", "vector", "scalar"):
                emit(eng, lambda e: e.nop(), waits=snap)

        for dst, src in ((IDENT, c_ident), (MCT, c_mc), (MLT, c_ml),
                         (O128, c_o128), (O1, c_o1)):
            emit("sync", lambda eng, d=dst, s=src: eng.dma_start(d[:], s.ap()),
                 inc="d_tab", ninc=16)
        emit("sync", lambda eng: eng.dma_start(
            N1W[:].rearrange("p (c one) -> p c one", one=1),
            dn1.rearrange("(c p) one -> p c one", p=128),
            allow_slow_non_contiguous=True), inc="d_tab", ninc=16)
        emit("sync", lambda eng: eng.dma_start(
            N2W[:].rearrange("p (c one) -> p c one", one=1),
            dn2.rearrange("(c p) one -> p c one", p=128),
            allow_slow_non_contiguous=True), inc="d_tab", ninc=16)
        emit("gpsimd", lambda eng: eng.dma_start(
            RWT[:].rearrange("p (c f) -> p c f", f=8),
            drwT.rearrange("(c p) f -> p c f", p=128)), inc="d_rwt", ninc=16)

        def gen_consts(eng):
            eng.memset(CM05[:], -0.5)
            eng.memset(ZCOL[:], 0.0)
            for col, v in ((0, 1e-5), (1, 1e-6), (2, 1.0 / 1024), (3, 1.0 / 64),
                           (4, 0.125), (5, 0.5), (6, 1.0), (7, -1.0)):
                eng.memset(CC[:, col:col + 1], v)
            last = None
            for T in range(16):
                for g in range(4):
                    o = 260 * T + 65 * g + 64
                    last = eng.memset(VA[:, o:o + 1], 1.0)
            return last
        emit("vector", gen_consts, waits=[W("d_tab")], inc="v_cst")

        def wload(dst_c0, w_ap, kdim, ndim, sem="d_w", waits=()):
            nch = kdim // 128
            per = max(1, 2048 // ndim)
            first = True
            for c0 in range(0, nch, per):
                c1 = min(nch, c0 + per)
                def f(eng, dst_c0=dst_c0, w_ap=w_ap, ndim=ndim, c0=c0, c1=c1):
                    return eng.dma_start(
                        BW[:, dst_c0 + c0 * ndim:dst_c0 + c1 * ndim]
                          .rearrange("p (c n) -> p c n", n=ndim),
                        w_ap[128 * c0:128 * c1, :]
                        .rearrange("(c p) n -> p c n", p=128))
                emit("gpsimd", f, waits=waits if first else (), inc=sem, ninc=16)
                first = False

        wload(0, dgwq, D, D)
        wload(8192, dgkvd, D, LAT)
        wload(10240, dgkup, LAT, LAT)
        wload(10752, dgvup, LAT, LAT)
        WLD = {"g": counts["d_w"]}

        for t in range(16):
            sl = (t % 2) * 1024
            wt = [W("pe_xtp", t - 1)] if t >= 2 else []
            emit("gpsimd", lambda eng, t=t, sl=sl: eng.dma_start(
                XST[:, sl:sl + 1024], dx[128 * t:128 * t + 128, :]),
                waits=wt, inc="d_x%d" % (t % 2), ninc=16)
        for e in range(E):
            for k4 in range(4):
                emit("gpsimd", lambda eng, e=e, k4=k4: eng.dma_start(
                    dw1b[e, 256 * k4:256 * k4 + 256, :],
                    dw1[e, 256 * k4:256 * k4 + 256, :]), inc="d_conv", ninc=16)
                emit("gpsimd", lambda eng, e=e, k4=k4: eng.dma_start(
                    dw2b[e, 512 * k4:512 * k4 + 512, :],
                    dw2[e, 512 * k4:512 * k4 + 512, :]), inc="d_conv", ninc=16)
        for t in range(16):
            sl = (t % 2) * 1024
            def ftp(eng, sl=sl):
                for c in range(NCH):
                    i = eng.transpose(PST[:, 128 * c:128 * c + 128],
                                      XST[:, sl + 128 * c:sl + 128 * c + 128],
                                      IDENT[:])
                return i
            emit("tensor", ftp,
                 waits=[W("d_tab", 112), W("d_x%d" % (t % 2), 16 * (t // 2 + 1))],
                 reads=("xst%d" % (t % 2),), writes=("pst",), inc="pe_xtp")
            def fev(eng, t=t):
                for c in range(NCH):
                    i = eng.tensor_copy(BA[:, fc(c, 128 * t):fc(c, 128 * t) + 128],
                                        PST[:, 128 * c:128 * c + 128])
                return i
            emit("vector", fev, waits=[W("pe_xtp")], reads=("pst",), inc="v_xfm")

        r0, r1, r2 = ROWS[0:1, 0:256], ROWS[0:1, 256:512], ROWS[0:1, 512:768]
        for sp in range(8):
            t0 = 256 * sp
            xfm_done = W("v_xfm", 2 * (sp + 1))
            def fsum(eng, t0=t0):
                for c in range(NCH):
                    i = eng.matmul(PSD[0:1, 0:256], O128[:],
                                   BA[:, fc(c, t0):fc(c, t0) + 256],
                                   start=(c == 0), stop=(c == NCH - 1))
                return i
            emit("tensor", fsum, waits=[xfm_done], writes=("psd",), inc="pe_lnsum")
            for c in range(NCH):
                sl = (c % 2) * 256
                def fsq(eng, c=c, sl=sl, t0=t0):
                    src = BA[:, fc(c, t0):fc(c, t0) + 256]
                    return eng.tensor_tensor(XSQ[:, sl:sl + 256], src, src, ALU.mult)
                emit("gpsimd", fsq, waits=[xfm_done],
                     reads=(), writes=("xsq%d" % (c % 2),), inc="po_sq")
                def fss(eng, c=c):
                    sl = (c % 2) * 256
                    return eng.matmul(PSC[0:1, 0:256], O128[:], XSQ[:, sl:sl + 256],
                                      start=(c == 0), stop=(c == NCH - 1))
                emit("tensor", fss, waits=[W("po_sq", 8 * sp + c + 1)],
                     reads=("xsq%d" % (c % 2),),
                     writes=("psc",) if c == 0 else (), inc="pe_lnss")
            emit("vector", lambda eng: eng.tensor_scalar(
                r0, PSD[0:1, 0:256], CC[0:1, 2:3], None, ALU.mult),
                waits=[W("pe_lnsum"), W("v_cst", 1)], reads=("psd",),
                inc="v_lnm")
            emit("vector", lambda eng: eng.tensor_tensor(r1, r0, r0, ALU.mult),
                 waits=[W("v_lnm")], inc="v_lnmsq")
            emit("vector", lambda eng: eng.scalar_tensor_tensor(
                r2, PSC[0:1, 0:256], CC[0:1, 2:3], r1, ALU.mult, ALU.subtract),
                waits=[W("pe_lnss"), W("v_lnmsq")], reads=("psc",),
                inc="v_lnvar")
            emit("scalar", lambda eng: eng.activation(
                r1, r2, AF.Ln, bias=CC[0:1, 0:1], scale=1.0),
                waits=[W("v_lnvar")], inc="sc_ln1")
            emit("scalar", lambda eng: eng.activation(
                r2, r1, AF.Exp, bias=0.0, scale=CM05[0:1, :]),
                waits=[W("sc_ln1")], inc="sc_lnsd")
            emit("vector", lambda eng: eng.tensor_tensor(r1, r0, r2, ALU.mult),
                 waits=[W("sc_lnsd")], inc="v_lnrow")
            def fbc(eng):
                eng.matmul(PSA[:, 0:256], O1[:], r2, start=True, stop=True)
                return eng.matmul(PSA[:, 512:768], O1[:], r1, start=True, stop=True)
            emit("tensor", fbc, waits=[W("v_lnrow")], writes=("psa",), inc="pe_lnbc")
            for c in range(NCH):
                def fxn(eng, c=c, t0=t0):
                    src = BA[:, fc(c, t0):fc(c, t0) + 256]
                    eng.tensor_tensor(XNT[:], src, PSA[:, 0:256], ALU.mult)
                    return eng.tensor_tensor(XN[:, fc(c, t0):fc(c, t0) + 256],
                                             XNT[:], PSA[:, 512:768], ALU.subtract)
                emit("vector", fxn, waits=[W("pe_lnbc")],
                     reads=("psa",) if c == NCH - 1 else (), inc="v_xn")

        def gen_prep(phase):
            srcbuf = XN if phase == "g" else BA
            wcol = N1W if phase == "g" else N2W
            qdst = BA if phase == "g" else BB
            ro = 0 if phase == "g" else 1024
            sdrow, rrow = PRR[0:1, 256:512], PRR[0:1, 0:256]
            for pt in range(8):
                t256 = 256 * pt
                srcrdy = (W("v_xn", 8 * (pt + 1)) if phase == "g"
                          else W("v_h1", 8 * (pt // 2 + 1)))
                for c in range(NCH):
                    sl = (c % 2) * 256
                    def fsq(eng, c=c, sl=sl, t256=t256, srcbuf=srcbuf):
                        src = srcbuf[:, fc(c, t256):fc(c, t256) + 256]
                        return eng.tensor_tensor(XSQ[:, sl:sl + 256], src, src,
                                                 ALU.mult)
                    emit("gpsimd", fsq, waits=[srcrdy],
                         writes=("xsq%d" % (c % 2),), inc="po_psq")
                    def fss(eng, c=c):
                        sl = (c % 2) * 256
                        return eng.matmul(PSD[0:1, 0:256], O128[:], XSQ[:, sl:sl + 256],
                                          start=(c == 0), stop=(c == NCH - 1))
                    emit("tensor", fss, waits=[W("po_psq")],
                         reads=("xsq%d" % (c % 2),),
                         writes=("psd",) if c == 0 else (), inc="pe_pss")
                emit("scalar", lambda eng, sdrow=sdrow: eng.activation(
                    sdrow, PSD[0:1, 0:256], AF.Ln,
                    bias=CC[0:1, 1:2], scale=CC[0:1, 2:3]),
                    waits=[W("pe_pss")], reads=("psd",), inc="sc_psd0")
                emit("scalar", lambda eng, sdrow=sdrow, rrow=rrow: eng.activation(
                    rrow, sdrow, AF.Exp, bias=0.0, scale=CM05[0:1, :]),
                    waits=[W("sc_psd0")], inc="sc_psd")
                emit("tensor", lambda eng, rrow=rrow: eng.matmul(
                    PSC[:, 0:256], O1[:], rrow, start=True, stop=True),
                    waits=[W("sc_psd")], writes=("psc",), inc="pe_pbc")
                for c in range(NCH):
                    def fa(eng, c=c, t256=t256, srcbuf=srcbuf, wcol=wcol):
                        return eng.scalar_tensor_tensor(
                            ATILE[:, 256 * c:256 * c + 256],
                            srcbuf[:, fc(c, t256):fc(c, t256) + 256],
                            wcol[:, c:c + 1], PSC[:, 0:256], ALU.mult, ALU.mult)
                    emit("vector", fa, waits=[W("pe_pbc")],
                         reads=("psc",) if c == NCH - 1 else (),
                         writes=("atile",), inc="v_atile")
                atile_rdy = W("v_atile")
                if PREP_SUB < 1:
                    continue
                if phase == "g":
                    def flat(eng):
                        i = None
                        for lc in range(2):
                            for c in range(NCH):
                                i = eng.matmul(
                                    PSB[:, 512 * lc:512 * lc + 256],
                                    BW[:, 8192 + 256 * c + 128 * lc:
                                       8192 + 256 * c + 128 * lc + 128],
                                    ATILE[:, 256 * c:256 * c + 256],
                                    start=(c == 0), stop=(c == NCH - 1))
                        return i
                    emit("tensor", flat,
                         waits=[atile_rdy, W("d_w", WLD["g"])],
                         reads=("atile",), writes=("psb",), inc="pe_lat")
                    if PREP_SUB >= 1.5:
                        def flatev(eng):
                            eng.tensor_copy(LATS[:, 0:256], PSB[:, 0:256])
                            return eng.tensor_copy(LATS[:, 256:512], PSB[:, 512:768])
                        emit("vector", flatev, waits=[W("pe_lat")],
                             reads=("psb",), writes=("lats",), inc="v_latev")
                for j in range(2):
                    if PREP_SUB < 2:
                        continue
                    T = 2 * pt + j
                    qp, qslot = (PSA, "psa") if j == 0 else (PSB, "psb")
                    def fq(eng, j=j, qp=qp):
                        i = None
                        for c in range(NCH):
                            for u in range(2):
                                i = eng.matmul(
                                    qp[:, 512 * u:512 * u + 512],
                                    ATILE[:, 256 * c + 128 * j:256 * c + 128 * j + 128],
                                    BW[:, 1024 * c + 512 * u:1024 * c + 512 * u + 512],
                                    start=(c == 0), stop=(c == NCH - 1))
                        return i
                    qw = [atile_rdy,
                          (W("d_w", WLD["g"]) if phase == "g" else W("d_wlq", 64))]
                    emit("tensor", fq,
                         waits=qw, reads=("atile",), writes=(qslot,), inc="pe_q")
                    rsl = 1024 * (T % 2)
                    rsem = "d_rope%d" % (T % 2)
                    def frld(eng, T=T, rsl=rsl, phase=phase):
                        return eng.dma_start(
                            ROPE[:, rsl:rsl + 1024],
                            (c_repg if phase == "g" else c_repl)
                            .ap()[128 * T:128 * T + 128, :])
                    emit("sync", frld, writes=("rope%d" % (T % 2),),
                         inc=rsem, ninc=16)
                    rld = counts[rsem]
                    def fqro(eng, qp=qp, rsl=rsl):
                        cosS = ROPE[:, rsl:rsl + 512]
                        sinS = ROPE[:, rsl + 512:rsl + 1024]
                        e, o = qp[:, 0:512], qp[:, 512:1024]
                        eng.tensor_tensor(RSC1[:], e, cosS, ALU.mult)
                        eng.tensor_tensor(RSC2[:], o, sinS, ALU.mult)
                        eng.tensor_tensor(QRO[:, 0:512], RSC1[:], RSC2[:],
                                          ALU.subtract)
                        eng.tensor_tensor(RSC1[:], e, sinS, ALU.mult)
                        eng.tensor_tensor(RSC2[:], o, cosS, ALU.mult)
                        return eng.tensor_tensor(QRO[:, 512:1024], RSC1[:], RSC2[:],
                                                 ALU.add)
                    emit("vector", fqro, waits=[W("pe_q"), (rsem, rld)],
                         reads=(qslot, "rope%d" % (T % 2)), writes=("qro",),
                         inc="v_qro2")
                    emit("gpsimd", lambda eng: eng.tensor_tensor(
                        QSQ[:], QRO[:], QRO[:], ALU.mult),
                        waits=[W("v_qro2")], reads=("qro",), writes=("qsq",),
                        inc="po_qsq")
                    def fqss(eng):
                        i = None
                        qv = QSQ.ap().rearrange("p (half rest) -> p half rest",
                                                half=2)
                        for h in range(H):
                            i = eng.reduce_sum(SMALLT[:, 64 + h:65 + h],
                                               qv[:, :, 32 * h:32 * h + 32],
                                               axis=mybir.AxisListType.XY)
                        return i
                    emit("vector", fqss, waits=[W("po_qsq")], reads=("qsq",),
                         writes=("smallt",), inc="v_qss")
                    emit("scalar", lambda eng: eng.activation(
                        SMALLT[:, 80:96], SMALLT[:, 64:80], AF.Sqrt,
                        bias=CC[:, 1:2], scale=CC[:, 3:4]),
                        waits=[W("v_qss")], reads=("smallt",), inc="sc_qsd")
                    emit("vector", lambda eng: eng.reciprocal(
                        SMALLT[:, 64:80], SMALLT[:, 80:96]),
                        waits=[W("sc_qsd")], inc="v_qrc")
                    def fqn(eng):
                        i = None
                        for h in range(H):
                            cn = 128 * ((h % 4) + 4 * (h // 8)) + 64 * ((h // 4) % 2)
                            i = eng.tensor_scalar(QN[:, cn:cn + 32],
                                                  QRO[:, 32 * h:32 * h + 32],
                                                  SMALLT[:, 64 + h:65 + h], None,
                                                  ALU.mult)
                            i = eng.tensor_scalar(QN[:, cn + 32:cn + 64],
                                                  QRO[:, 512 + 32 * h:512 + 32 * h + 32],
                                                  SMALLT[:, 64 + h:65 + h], None,
                                                  ALU.mult)
                        return i
                    emit("vector", fqn, waits=[W("v_qrc")], reads=("qro",),
                         writes=("qn",), inc="v_qn")
                    def fqtp(eng):
                        i = None
                        for c in range(NCH):
                            i = eng.transpose(PST[:, 128 * c:128 * c + 128],
                                              QN[:, 128 * c:128 * c + 128], IDENT[:])
                        return i
                    emit("tensor", fqtp, waits=[W("v_qn")],
                         reads=("qn",), writes=("pst",), inc="pe_ptp")
                    def fqev(eng, T=T, qdst=qdst):
                        i = None
                        for c in range(NCH):
                            i = eng.tensor_copy(
                                qdst[:, fc(c, 128 * T):fc(c, 128 * T) + 128],
                                PST[:, 128 * c:128 * c + 128])
                        return i
                    emit("vector", fqev, waits=[W("pe_ptp")], reads=("pst",),
                         inc="v_ptev")
                    if PREP_SUB < 3:
                        continue
                    def fk(eng, j=j, phase=phase):
                        i = None
                        if phase == "g":
                            for lc in range(2):
                                i = eng.matmul(
                                    PSD[:, 0:256],
                                    LATS[:, 256 * lc + 128 * j:256 * lc + 128 * j + 128],
                                    BW[:, 10240 + 256 * lc:10240 + 256 * lc + 256],
                                    start=(lc == 0), stop=(lc == 1))
                        else:
                            for c in range(NCH):
                                i = eng.matmul(
                                    PSD[:, 0:256],
                                    ATILE[:, 256 * c + 128 * j:256 * c + 128 * j + 128],
                                    BW[:, 8192 + 256 * c:8192 + 256 * c + 256],
                                    start=(c == 0), stop=(c == NCH - 1))
                        return i
                    kwaits = ([W("v_latev")] if phase == "g"
                              else [atile_rdy, W("d_wlk", 16)])
                    emit("tensor", fk, waits=kwaits,
                         reads=("lats",) if phase == "g" else ("atile",),
                         writes=("psd",), inc="pe_k")
                    def fkro(eng, rsl=rsl):
                        cosS = ROPE[:, rsl:rsl + 128]
                        sinS = ROPE[:, rsl + 512:rsl + 640]
                        e, o = PSD[:, 0:128], PSD[:, 128:256]
                        eng.tensor_tensor(RSC1[:, 0:128], e, cosS, ALU.mult)
                        eng.tensor_tensor(RSC2[:, 0:128], o, sinS, ALU.mult)
                        eng.tensor_tensor(KRO[:, 0:128], RSC1[:, 0:128],
                                          RSC2[:, 0:128], ALU.subtract)
                        eng.tensor_tensor(RSC1[:, 0:128], e, sinS, ALU.mult)
                        eng.tensor_tensor(RSC2[:, 0:128], o, cosS, ALU.mult)
                        return eng.tensor_tensor(KRO[:, 128:256], RSC1[:, 0:128],
                                                 RSC2[:, 0:128], ALU.add)
                    emit("vector", fkro, waits=[W("pe_k"), (rsem, rld)],
                         reads=("psd", "rope%d" % (T % 2)), writes=("kro",),
                         inc="v_kro2")
                    emit("gpsimd", lambda eng: eng.tensor_tensor(
                        KSQ[:], KRO[:], KRO[:], ALU.mult),
                        waits=[W("v_kro2")], reads=("kro",), writes=("ksq",),
                        inc="po_ksq")
                    def fkss(eng):
                        i = None
                        kv2 = KSQ.ap().rearrange("p (half rest) -> p half rest",
                                                 half=2)
                        for g in range(KVH):
                            i = eng.reduce_sum(SMALLT[:, 64 + g:65 + g],
                                               kv2[:, :, 32 * g:32 * g + 32],
                                               axis=mybir.AxisListType.XY)
                        return i
                    emit("vector", fkss, waits=[W("po_ksq")], reads=("ksq",),
                         writes=("smallt",), inc="v_kss")
                    emit("scalar", lambda eng: eng.activation(
                        SMALLT[:, 80:84], SMALLT[:, 64:68], AF.Sqrt,
                        bias=CC[:, 1:2], scale=CC[:, 3:4]),
                        waits=[W("v_kss")], reads=("smallt",), inc="sc_ksd")
                    emit("vector", lambda eng: eng.reciprocal(
                        SMALLT[:, 64:68], SMALLT[:, 80:84]),
                        waits=[W("sc_ksd")], inc="v_krc")
                    def fkn(eng):
                        i = None
                        for g in range(KVH):
                            i = eng.tensor_scalar(KN[:, 64 * g:64 * g + 32],
                                                  KRO[:, 32 * g:32 * g + 32],
                                                  SMALLT[:, 64 + g:65 + g], None,
                                                  ALU.mult)
                            i = eng.tensor_scalar(KN[:, 64 * g + 32:64 * g + 64],
                                                  KRO[:, 128 + 32 * g:128 + 32 * g + 32],
                                                  SMALLT[:, 64 + g:65 + g], None,
                                                  ALU.mult)
                        return i
                    emit("vector", fkn, waits=[W("v_krc")], reads=("kro",),
                         writes=("kn",), inc="v_kn")
                    def fktp(eng):
                        eng.transpose(PST[:, 0:128], KN[:, 0:128], IDENT[:])
                        return eng.transpose(PST[:, 128:256], KN[:, 128:256], IDENT[:])
                    emit("tensor", fktp, waits=[W("v_kn")],
                         reads=("kn",), writes=("pst",), inc="pe_ptp")
                    def fkev(eng, T=T):
                        eng.tensor_copy(K0[:, 128 * T:128 * T + 128], PST[:, 0:128])
                        return eng.tensor_copy(K1[:, 128 * T:128 * T + 128],
                                               PST[:, 128:256])
                    emit("vector", fkev, waits=[W("pe_ptp")], reads=("pst",),
                         inc="v_ptev")
                    if PREP_SUB < 4:
                        continue
                    def fv(eng, j=j, phase=phase):
                        i = None
                        if phase == "g":
                            for lc in range(2):
                                i = eng.matmul(
                                    PSD[:, 256:512],
                                    LATS[:, 256 * lc + 128 * j:256 * lc + 128 * j + 128],
                                    BW[:, 10752 + 256 * lc:10752 + 256 * lc + 256],
                                    start=(lc == 0), stop=(lc == 1))
                        else:
                            for c in range(NCH):
                                i = eng.matmul(
                                    PSD[:, 256:512],
                                    ATILE[:, 256 * c + 128 * j:256 * c + 128 * j + 128],
                                    BW[:, 11264 + 256 * c:11264 + 256 * c + 256],
                                    start=(c == 0), stop=(c == NCH - 1))
                        return i
                    vwaits = [W("v_kro2")]
                    vwaits.append(W("v_latev") if phase == "g" else atile_rdy)
                    if phase == "l":
                        vwaits.append(W("d_wlv", 16))
                    emit("tensor", fv, waits=vwaits,
                         reads=("lats",) if phase == "g" else ("atile",),
                         writes=("psd2",), inc="pe_v")
                    def fvev(eng, T=T):
                        i = None
                        for g in range(KVH):
                            i = eng.tensor_copy(
                                VA[:, 260 * T + 65 * g:260 * T + 65 * g + 64],
                                PSD[:, 256 + 64 * g:256 + 64 * g + 64])
                        return i
                    emit("vector", fvev, waits=[W("pe_v"), W("v_cst", 1)],
                         reads=("psd2", "psd"), inc="v_vev")
                    guards["psd"] = ("v_vev", counts["v_vev"])

        def gen_attn(phase):
            qbuf = BA if phase == "g" else BB
            obuf = BB if phase == "g" else BC
            for h in range(H):
                kv = h // 4
                ktens = K0 if kv < 2 else K1
                krow = 64 * (kv % 2)
                qch = (h % 4) + 4 * (h // 8)
                qrow = 64 * ((h // 4) % 2)
                for qb in range(2):
                    qlo = 1024 * qb
                    steps = []
                    for kb in range(S // 128):
                        k0 = 128 * kb
                        if phase == "g":
                            lo = max(qlo, k0)
                            if lo >= qlo + 1024:
                                continue
                            j0, j1 = lo - qlo, 1024
                            if k0 >= qlo:
                                jm0, jm1 = j0, min(j0 + 128, j1)
                            else:
                                jm0 = jm1 = j0
                        else:
                            lo, hi = max(qlo, k0), min(qlo + 1024, k0 + 128 + WIN)
                            if lo >= hi:
                                continue
                            j0, j1 = lo - qlo, hi - qlo
                            jm0, jm1 = j0, j1
                        steps.append((kb, j0, j1, jm0, jm1))
                    nst = len(steps)
                    if phase == "l":
                        def fzero(eng):
                            eng.matmul(PSC[0:65, 0:512], ZCOL[:], K0[:, 0:512],
                                       start=True, stop=False)
                            return eng.matmul(PSC[0:65, 512:1024], ZCOL[:],
                                              K0[:, 0:512], start=True, stop=False)
                        emit("tensor", fzero, waits=[W("v_cst", 1)],
                             writes=("psc",), inc="pe_pvz")
                    def mk_fsc(kb, j0, j1, scp):
                        def fsc(eng, kb=kb, j0=j0, j1=j1, scp=scp, ktens=ktens,
                                krow=krow, qch=qch, qrow=qrow, qlo=qlo, qbuf=qbuf):
                            i = None
                            for (a, b) in _col_segs(j0, j1):
                                i = eng.matmul(
                                    scp[:, a:b],
                                    ktens[krow:krow + 64, 128 * kb:128 * kb + 128],
                                    qbuf[qrow:qrow + 64,
                                         fc(qch, qlo + a):fc(qch, qlo + a) + (b - a)],
                                    start=True, stop=True)
                            return i
                        return fsc
                    sccnt = {}
                    def emit_sc(si):
                        kb, j0, j1, _, _ = steps[si]
                        scslot = "psa" if kb % 2 == 0 else "psb"
                        scp = PSA if kb % 2 == 0 else PSB
                        emit("tensor", mk_fsc(kb, j0, j1, scp), writes=(scslot,),
                             inc="pe_sc")
                        sccnt[si] = counts["pe_sc"]
                    emit_sc(0)
                    for si, (kb, j0, j1, jm0, jm1) in enumerate(steps):
                        scp, scslot = (PSA, "psa") if kb % 2 == 0 else (PSB, "psb")
                        pp0 = (kb % 2) * 1024
                        ppslot = "probs%d" % (kb % 2)
                        if si + 1 < len(steps):
                            emit_sc(si + 1)
                        if jm1 > jm0 and jm1 < j1:
                            mt = MCT if phase == "g" else MLT
                            mo = qlo + jm0 - 128 * kb
                            def fmask(eng, jm0=jm0, jm1=jm1, scp=scp, mt=mt, mo=mo):
                                eng.tensor_tensor(
                                    MSCR[:, 0:jm1 - jm0], scp[:, jm0:jm1],
                                    mt[:, mo:mo + jm1 - jm0], ALU.add)
                                return eng.tensor_copy(scp[:, jm0:jm1],
                                                       MSCR[:, 0:jm1 - jm0])
                            emit("vector", fmask,
                                 waits=[W("pe_sc", sccnt[si]), W("d_tab", 112)],
                                 writes=("mscr",), inc="v_msk")
                            def fexp(eng, j0=j0, j1=j1, scp=scp, pp0=pp0):
                                return eng.activation(
                                    PROBS[:, pp0 + j0:pp0 + j1], scp[:, j0:j1],
                                    AF.Exp, bias=0.0, scale=CC[:, 4:5])
                            emit("scalar", fexp,
                                 waits=[W("pe_sc", sccnt[si]), W("v_msk")],
                                 reads=(scslot, "mscr"), writes=(ppslot,),
                                 inc="sc_exp")
                        elif jm1 > jm0:
                            mt = MCT if phase == "g" else MLT
                            mo = qlo + jm0 - 128 * kb
                            def fmask(eng, jm0=jm0, jm1=jm1, scp=scp, mt=mt, mo=mo):
                                return eng.tensor_tensor(
                                    MSCR[:, 0:jm1 - jm0], scp[:, jm0:jm1],
                                    mt[:, mo:mo + jm1 - jm0], ALU.add)
                            emit("vector", fmask,
                                 waits=[W("pe_sc", sccnt[si]), W("d_tab", 112)],
                                 writes=("mscr",), inc="v_msk")
                            def fexp(eng, jm0=jm0, jm1=jm1, pp0=pp0):
                                return eng.activation(
                                    PROBS[:, pp0 + jm0:pp0 + jm1],
                                    MSCR[:, 0:jm1 - jm0], AF.Exp,
                                    bias=0.0, scale=CC[:, 4:5])
                            emit("scalar", fexp, waits=[W("v_msk")],
                                 reads=(scslot, "mscr"), writes=(ppslot,),
                                 inc="sc_exp")
                        else:
                            def fexp(eng, j0=j0, j1=j1, scp=scp, pp0=pp0):
                                return eng.activation(
                                    PROBS[:, pp0 + j0:pp0 + j1], scp[:, j0:j1],
                                    AF.Exp, bias=0.0, scale=CC[:, 4:5])
                            emit("scalar", fexp, waits=[W("pe_sc", sccnt[si])],
                                 reads=(scslot,), writes=(ppslot,), inc="sc_exp")
                        def fpv(eng, kb=kb, si=si, steps=tuple(steps), pp0=pp0,
                                kv=kv):
                            i = None
                            for (a, b) in _col_segs(steps[si][1], steps[si][2]):
                                bank = 0 if a < 512 else 1
                                first = True
                                for s2 in steps[:si]:
                                    if max(s2[1], 512 * bank) < min(s2[2],
                                                                    512 * bank + 512):
                                        first = False
                                        break
                                last = True
                                for s2 in steps[si + 1:]:
                                    if max(s2[1], 512 * bank) < min(s2[2],
                                                                    512 * bank + 512):
                                        last = False
                                        break
                                i = eng.matmul(
                                    PSC[0:65, a:b],
                                    VA[:, 260 * kb + 65 * kv:260 * kb + 65 * kv + 65],
                                    PROBS[:, pp0 + a:pp0 + b],
                                    start=(first and phase == "g"), stop=last)
                            return i
                        emit("tensor", fpv, waits=[W("sc_exp")],
                             reads=(ppslot,),
                             writes=("psc",) if si == 0 else (), inc="pe_pv")
                    emit("scalar", lambda eng: eng.activation(
                        RDR[:], PSC[64:65, :], AF.Ln, bias=0.0, scale=1.0),
                        waits=[W("pe_pv")], writes=("rdr",), inc="v_rd")
                    for half in range(2):
                        def fbcd(eng, half=half):
                            return eng.matmul(PSD[:, 0:512], O1[:],
                                              RDR[0:1, 512 * half:512 * half + 512],
                                              start=True, stop=True)
                        emit("tensor", fbcd, waits=[W("v_rd")],
                             reads=("rdr",) if half == 1 else (),
                             writes=("psd",), inc="pe_rbc")
                        emit("scalar", lambda eng: eng.activation(
                            RBCS[:], PSD[:, 0:512], AF.Exp, bias=0.0,
                            scale=CC[:, 7:8]),
                            waits=[W("pe_rbc")], reads=("psd",), writes=("rbcs",),
                            inc="sc_rbc")
                        def faev(eng, half=half, qrow=qrow, qch=qch, qlo=qlo,
                                 obuf=obuf):
                            return eng.tensor_tensor(
                                obuf[qrow:qrow + 64,
                                     fc(qch, qlo + 512 * half):
                                     fc(qch, qlo + 512 * half) + 512],
                                PSC[0:64, 512 * half:512 * half + 512],
                                RBCS[0:64, :], ALU.mult)
                        emit("vector", faev, waits=[W("sc_rbc")],
                             reads=("psc", "rbcs") if half == 1 else ("rbcs",),
                             inc="v_aev")

        def gen_wo(phase):
            src = BB if phase == "g" else BC
            res = XN if phase == "g" else BA
            dst = BA if phase == "g" else BB
            incname = "v_h1" if phase == "g" else "v_s"
            wosem = "d_wgo" if phase == "g" else "d_wlo"
            for w4 in range(4):
                t0 = 512 * w4
                for mc in range(NCH):
                    half = mc % 2
                    slot = "wops%d" % half
                    def fwo(eng, mc=mc, t0=t0, half=half, src=src):
                        i = None
                        for c in range(NCH):
                            i = eng.matmul(
                                PSA[:, 512 * half:512 * half + 512],
                                BW[:, 1024 * c + 128 * mc:1024 * c + 128 * mc + 128],
                                src[:, fc(c, t0):fc(c, t0) + 512],
                                start=(c == 0), stop=(c == NCH - 1))
                        return i
                    emit("tensor", fwo, waits=[W(wosem, 64)],
                         writes=(slot,), inc="pe_wo")
                    def fh1(eng, mc=mc, t0=t0, half=half, res=res, dst=dst):
                        return eng.scalar_tensor_tensor(
                            dst[:, fc(mc, t0):fc(mc, t0) + 512],
                            PSA[:, 512 * half:512 * half + 512], 1.0,
                            res[:, fc(mc, t0):fc(mc, t0) + 512],
                            ALU.bypass, ALU.add)
                    emit("vector", fh1, waits=[W("pe_wo")], reads=(slot,),
                         inc=incname)

        if stage >= 2:
            phase_guard("prep_g")
            gen_prep("g")
        if stage >= 3:
            wload(0, dgwo, D, D, sem="d_wgo", waits=[W("pe_q")])
            wload(8192, dlwk, D, LAT, sem="d_wlk", waits=[W("pe_lat")])
            wload(11264, dlwv, D, LAT, sem="d_wlv", waits=[W("pe_v")])
            phase_guard("attn_g")
            gen_attn("g")
        if stage >= 4:
            phase_guard("wo_g")
            gen_wo("g")
            wload(0, dlwq, D, D, sem="d_wlq", waits=[W("pe_wo")])
        if stage >= 5:
            phase_guard("prep_l")
            gen_prep("l")
        if stage >= 6:
            wload(0, dlwo, D, D, sem="d_wlo", waits=[W("pe_q")])
            phase_guard("attn_l")
            gen_attn("l")
        if stage >= 7:
            phase_guard("wo_l")
            gen_wo("l")

        if stage >= 8:
            emit("sync", lambda eng: eng.dma_start(IND8[:], c_ind8.ap()),
                 waits=[W("v_s")], inc="d_ind8", ninc=16)
            phase_guard("router")
            for T in range(16):
                def flg(eng, T=T):
                    i = None
                    for c in range(NCH):
                        i = eng.matmul(PSD[:, 0:8],
                                       XN[:, fc(c, 128 * T):fc(c, 128 * T) + 128],
                                       RWT[:, 8 * c:8 * c + 8],
                                       start=(c == 0), stop=(c == NCH - 1))
                    return i
                emit("tensor", flg, waits=[W("d_rwt", 16)], writes=("psd",),
                     inc="pe_lg")
                emit("vector", lambda eng: eng.tensor_copy(LGT[:, 0:8],
                                                           PSD[:, 0:8]),
                     waits=[W("pe_lg")], reads=("psd",), inc="v_lgc")
                def ftm(eng):
                    eng.tensor_copy(LGT[:, 8:9], LGT[:, 0:1])
                    eng.memset(LGT[:, 16:17], -1e30)
                    i = None
                    for ci in range(1, 8):
                        pa, pb = LGT[:, 7 + ci:8 + ci], LGT[:, 15 + ci:16 + ci]
                        na, nb = LGT[:, 8 + ci:9 + ci], LGT[:, 16 + ci:17 + ci]
                        xcol = LGT[:, ci:ci + 1]
                        eng.tensor_tensor(LGT[:, 24:25], pa, xcol, ALU.min)
                        eng.tensor_tensor(na, pa, xcol, ALU.max)
                        i = eng.tensor_tensor(nb, pb, LGT[:, 24:25], ALU.max)
                    return i
                emit("vector", ftm, waits=[W("v_lgc")],
                     writes=("lgt",), inc="v_net")
                emit("vector", lambda eng: eng.tensor_scalar(
                    MKT[:], LGT[:, 0:8], LGT[:, 23:24], None, ALU.is_ge),
                    waits=[W("v_net")], inc="v_tm")
                emit("scalar", lambda eng: eng.activation(ZTM[:], LGT[:, 0:8],
                                                          AF.Exp, bias=0.0,
                                                          scale=1.0),
                     waits=[W("v_tm")], inc="sc_z")
                emit("vector", lambda eng: eng.tensor_tensor(
                    LGT[:, 32:40], ZTM[:], MKT[:], ALU.mult),
                    waits=[W("sc_z")], inc="v_zm")
                emit("vector", lambda eng: eng.reduce_sum(
                    LGT[:, 40:41], LGT[:, 32:40], axis=AX),
                    waits=[W("v_zm")], inc="v_dsum")
                emit("vector", lambda eng, T=T: eng.reciprocal(
                    RCPD[:, T:T + 1], LGT[:, 40:41]),
                    waits=[W("v_dsum")], reads=("lgt",), inc="v_route")
                emit("tensor", lambda eng: eng.transpose(PST[0:8, 0:128], MKT[:],
                                                         IDENT[:]),
                     waits=[W("v_route")], writes=("pst",), inc="pe_mtp")
                emit("vector", lambda eng, T=T: eng.tensor_copy(
                    MKF[0:8, 128 * T:128 * T + 128], PST[0:8, 0:128]),
                    waits=[W("pe_mtp")], reads=("pst",), inc="v_mkf")
            for sp in range(4):
                def flgf(eng, sp=sp):
                    i = None
                    for c in range(NCH):
                        i = eng.matmul(PSD[0:8, 0:512], RWT[:, 8 * c:8 * c + 8],
                                       XN[:, fc(c, 512 * sp):fc(c, 512 * sp) + 512],
                                       start=(c == 0), stop=(c == NCH - 1))
                    return i
                emit("tensor", flgf, waits=[W("v_mkf", 16)], writes=("psd",),
                     inc="pe_lgf")
                emit("scalar", lambda eng: eng.activation(
                    SRZ[:], PSD[0:8, 0:512], AF.Exp, bias=0.0, scale=CC[0:8, 5:6]),
                    waits=[W("pe_lgf")], reads=("psd",), writes=("srz",),
                    inc="sc_srz")
                def fszm(eng, sp=sp):
                    return eng.tensor_tensor(SZM[0:8, 512 * sp:512 * sp + 512],
                                             SRZ[:],
                                             MKF[0:8, 512 * sp:512 * sp + 512],
                                             ALU.mult)
                emit("vector", fszm, waits=[W("sc_srz")], reads=("srz",),
                     inc="v_szm")

        if stage >= 9:
            phase_guard("ffn")
            NU = 32
            for ps in range(8):
                t0 = 256 * ps
                pend_mm2, pend_hc = [], []
                for e2 in range(E):
                    def fwbc(eng, e2=e2, t0=t0):
                        return eng.matmul(
                            PSD[:, 0:256], IND8[:, 128 * e2:128 * e2 + 128],
                            SZM[0:8, t0:t0 + 256], start=True, stop=True)
                    emit("tensor", fwbc, waits=[W("v_szm", 4)],
                         writes=("psd",), inc="pe_wbc")
                    emit("scalar", lambda eng, e2=e2: eng.copy(
                        WBCS[:, 256 * e2:256 * e2 + 256], PSD[:, 0:256]),
                        waits=[W("pe_wbc")], reads=("psd",),
                        writes=("wbcs%d" % e2,), inc="sc_wbc")
                for u in range(NU):
                    e, qh = u // 4, u % 4
                    ub = 8192 * (u % 2)
                    bufslot = "bw%d" % (u % 2)
                    uw = [W("d_conv")]
                    if ps == 0 and u < 2:
                        uw += [W("pe_wo"), W("pe_q")]
                    def fw1(eng, e=e, qh=qh, ub=ub):
                        return eng.dma_start(
                            BW[:, ub:ub + 4096].rearrange("p (c n) -> p c n", n=512),
                            dw1b[e, :, 512 * qh:512 * qh + 512]
                            .rearrange("(c p) n -> p c n", p=128))
                    def fw2(eng, e=e, qh=qh, ub=ub):
                        return eng.dma_start(
                            BW[:, ub + 4096:ub + 8192]
                            .rearrange("p (c n) -> p c n", n=1024),
                            dw2b[e, 512 * qh:512 * qh + 512, :]
                            .rearrange("(c p) n -> p c n", p=128))
                    fsem = "d_fw%d" % (u % 2)
                    emit("sync", fw1, waits=uw, writes=(bufslot,),
                         inc=fsem, ninc=16)
                    emit("sync", fw2, waits=[], inc=fsem, ninc=16)
                    wld_u = counts[fsem]
                    p2cnt = {}
                    def emit_mm1(hc):
                        pcol = 512 * (hc % 2)
                        hslot = "ffnh%d" % (hc % 2)
                        def fmm1(eng, ub=ub, hc=hc, t0=t0, pcol=pcol):
                            i = None
                            for c in range(NCH):
                                i = eng.matmul(
                                    PSA[:, pcol:pcol + 256],
                                    BW[:, ub + 512 * c + 128 * hc:
                                       ub + 512 * c + 128 * hc + 128],
                                    XN[:, fc(c, t0):fc(c, t0) + 256],
                                    start=(c == 0), stop=(c == NCH - 1))
                            return i
                        emit("tensor", fmm1, waits=[(fsem, wld_u)],
                             reads=(bufslot,), writes=(hslot,), inc="pe_mm1")
                        mm1c = counts["pe_mm1"]
                        def fp1(eng, hc=hc, e=e, pcol=pcol):
                            return eng.scalar_tensor_tensor(
                                ASCR[:, 256 * (hc % 2):256 * (hc % 2) + 256],
                                PSA[:, pcol:pcol + 256], 0.0,
                                WBCS[:, 256 * e:256 * e + 256],
                                ALU.max, ALU.mult)
                        emit("vector", fp1,
                             waits=[W("pe_mm1", mm1c), W("sc_wbc")],
                             reads=(hslot, "wbcs%d" % e),
                             writes=("ascr%d" % (hc % 2),), inc="v_p1")
                        p1c = counts["v_p1"]
                        def fp2(eng, hc=hc):
                            a = ASCR[:, 256 * (hc % 2):256 * (hc % 2) + 256]
                            return eng.tensor_tensor(
                                HBF[:, 256 * hc:256 * hc + 256], a, a, ALU.mult)
                        emit("gpsimd", fp2, waits=[W("v_p1", p1c)],
                             reads=("ascr%d" % (hc % 2),),
                             writes=("hbf%d" % hc,), inc="po_p2")
                        p2cnt[hc] = counts["po_p2"]
                    def emit_mm2(hc, p2cnt=p2cnt, u=u, ub=ub, bufslot=bufslot):
                        def fmm2(eng, ub=ub, hc=hc, u=u):
                            i = None
                            for j in range(2):
                                yp = PSB if j == 0 else PSC
                                for uu in range(2):
                                    i = eng.matmul(
                                        yp[:, 512 * uu:512 * uu + 512],
                                        HBF[:, 256 * hc + 128 * j:
                                            256 * hc + 128 * j + 128],
                                        BW[:, ub + 4096 + 1024 * hc + 512 * uu:
                                           ub + 4096 + 1024 * hc + 512 * uu + 512],
                                        start=(u == 0 and hc == 0),
                                        stop=(u == NU - 1 and hc == 3))
                            return i
                        m2w = [W("po_p2", p2cnt[hc])]
                        if u == 0 and hc == 0:
                            m2w.append(("sc_g", counts["sc_g"]))
                        emit("tensor", fmm2, waits=m2w,
                             reads=("hbf%d" % hc, bufslot),
                             writes=("psbc",) if (u == 0 and hc == 0) else (),
                             inc="pe_mm2")
                    for hc in range(4):
                        emit_mm1(hc)
                        pend_mm2.append(emit_mm2)
                        if len(pend_mm2) > 2:
                            fn_u = pend_mm2.pop(0)
                            fn_u(pend_hc.pop(0))
                        pend_hc.append(hc)
                    while len(pend_mm2) > 2:
                        fn_u = pend_mm2.pop(0)
                        fn_u(pend_hc.pop(0))
                while pend_mm2:
                    fn_u = pend_mm2.pop(0)
                    fn_u(pend_hc.pop(0))
                for j in range(2):
                    T = 2 * ps + j
                    yp = PSB if j == 0 else PSC
                    emit("sync", lambda eng, T=T: eng.dma_start(
                        XRL[:], dx[128 * T:128 * T + 128, :]),
                        waits=[W("v_ob")], inc="d_xr", ninc=16)
                    def fg(eng, T=T, yp=yp):
                        return eng.activation(GSB[:], yp[:], AF.Silu, bias=0.0,
                                              scale=RCPD[:, T:T + 1])
                    emit("scalar", fg,
                         waits=[W("pe_mm2"), W("v_route", 16), W("d_out")],
                         reads=("psbc",), writes=("gsb",), inc="sc_g")
                    def fstp(eng, T=T):
                        i = None
                        for c in range(NCH):
                            i = eng.transpose(PST[:, 128 * c:128 * c + 128],
                                              BB[:, fc(c, 128 * T):fc(c, 128 * T)
                                                 + 128], IDENT[:])
                        return i
                    emit("tensor", fstp, waits=[W("v_s")], writes=("pst",),
                         inc="pe_stp")
                    def foa(eng):
                        return eng.tensor_tensor(OUTA[:], PST[:], GSB[:], ALU.mult)
                    emit("vector", foa, waits=[W("pe_stp"), W("sc_g")],
                         reads=("pst", "gsb"), writes=("outa",), inc="v_oa")
                    def fob(eng):
                        return eng.tensor_tensor(OUTB[:], OUTA[:], XRL[:], ALU.add)
                    emit("vector", fob, waits=[W("d_xr"), W("d_out")],
                         reads=("outa", "xrl"), writes=("outb",), inc="v_ob")
                    emit("sync", lambda eng, T=T: eng.dma_start(
                        dout[128 * T:128 * T + 128, :], OUTB[:]),
                        waits=[W("v_ob")], reads=("outb",), inc="d_out", ninc=16)

        tapsrc = dict(xn=XN, a1=ATILE, qg=BA, kg0=K0, kg1=K1, vg=VA, gout=BB,
                      h1=BA, ql=BB, lout=BC, s=BB, szm=SZM, mkf=MKF, rcpd=RCPD,
                      qro=QRO, qn=QN, smallt=SMALLT)
        if tapt:
            snap = [(s, c) for s, c in counts.items() if c > 0]
            for name, ap in tapt.items():
                src = tapsrc[name]
                def ftap(eng, ap=ap, src=src):
                    return eng.dma_start(
                        ap, src[0:ap.shape[0], 0:ap.shape[1]])
                emit("sync", ftap, waits=snap, inc="d_out", ninc=16)

        final_out = counts["d_out"]

        semnames = sorted({inc for (_, _, _, inc, _) in prog if inc}
                          | {s for (_, _, ws, _, _) in prog for (s, _) in ws})
        sems = {n: en(nc.semaphore(n)) for n in semnames}
        block = en(nc.Block())

        def runner(ename):
            def body(eng):
                for (engine, fn, waits, inc, ninc) in prog:
                    if engine != ename:
                        continue
                    for (s, v) in waits:
                        eng.wait_ge(sems[s], v)
                    last = fn(eng)
                    if inc is not None:
                        assert last is not None, (engine, inc)
                        last.then_inc(sems[inc], ninc)
                if ename == "sync" and final_out > 0:
                    eng.wait_ge(sems["d_out"], final_out)
            return body

        block.sync(runner("sync"))
        block.tensor(runner("tensor"))
        block.vector(runner("vector"))
        block.scalar(runner("scalar"))
        block.gpsimd(runner("gpsimd"))

    return nc, tapt


def _perm_eo(w, head=64):
    ncols = w.shape[-1]
    idx = np.arange(ncols).reshape(-1, head)
    out = np.concatenate([idx[:, 0::2].reshape(-1), idx[:, 1::2].reshape(-1)])
    return np.ascontiguousarray(w[..., out])


def qhead_perm():
    perm = np.zeros(1024, dtype=np.int64)
    for c in range(8):
        a = (c % 4) + 8 * (c // 4)
        for d in range(64):
            perm[128 * c + d] = 64 * a + d
            perm[128 * c + 64 + d] = 64 * (a + 4) + d
    return perm


def _perm_wo(w):
    return np.ascontiguousarray(np.asarray(w)[qhead_perm(), :])


def prep_inputs(inputs):
    w = {
        "n1w": np.ascontiguousarray(np.asarray(inputs["L0_attn_norm1_w"])
                                    .reshape(D, 1)),
        "n2w": np.ascontiguousarray(np.asarray(inputs["L0_attn_norm2_w"])
                                    .reshape(D, 1)),
        "gwq": _perm_eo(np.asarray(inputs["L0_attn_g_wq"])),
        "gkvd": np.ascontiguousarray(inputs["L0_attn_g_kv_down"]),
        "gkup": _perm_eo(np.asarray(inputs["L0_attn_g_k_up"])),
        "gvup": np.ascontiguousarray(inputs["L0_attn_g_v_up"]),
        "gwo": _perm_wo(inputs["L0_attn_g_wo"]),
        "lwq": _perm_eo(np.asarray(inputs["L0_attn_l_wq"])),
        "lwk": _perm_eo(np.asarray(inputs["L0_attn_l_wk"])),
        "lwv": np.ascontiguousarray(inputs["L0_attn_l_wv"]),
        "lwo": _perm_wo(inputs["L0_attn_l_wo"]),
        "rwT": np.ascontiguousarray(np.asarray(inputs["L0_ffn_router_w"]).T),
        "w1": np.ascontiguousarray(inputs["L0_ffn_w1"]),
        "w2": np.ascontiguousarray(inputs["L0_ffn_w2"]),
    }
    w = {k: np.asarray(v, dtype=np.float32) for k, v in w.items()}
    xs = np.asarray(inputs["L0_attn_hidden"], dtype=np.float32)
    return [dict(w, x=np.ascontiguousarray(xs[b])) for b in range(8)]


def kernel(**inputs):
    in_maps = prep_inputs(inputs)
    nc, _ = build_nc()
    res = run_bass_kernel_spmd(nc, in_maps, core_ids=list(range(8)))
    return np.stack([r["out"] for r in res.results], axis=0)


# version 05
# speedup vs baseline: 1.3104x; 1.3104x over previous
import numpy as np
import ml_dtypes
import concourse.bass as bass
import concourse.mybir as mybir
from concourse.bass_utils import run_bass_kernel_spmd
from contextlib import ExitStack
from collections import defaultdict

F32 = mybir.dt.float32
BF16 = mybir.dt.bfloat16
FP16 = mybir.dt.float16
AF = mybir.ActivationFunctionType
ALU = mybir.AluOpType
AX = mybir.AxisListType.X

S, D, NCH = 2048, 1024, 8
H, KVH, HD, LAT, WIN = 16, 4, 64, 256, 256
E, HF = 8, 2048
NEG = -30000.0

OFF_XN, OFF_A, OFF_B, OFF_C, OFF_W, OFF_K, OFF_T = (
    0, 32768, 65536, 98304, 131072, 163840, 196608)
ARENA = 208896
PREP_SUB = 9
TEST_WLOADS = True
TEST_ATTN = True
ENGINES = ("sync", "tensor", "vector", "scalar", "gpsimd")


def _host_constants():
    c = {}
    c["ident"] = np.eye(128, dtype=ml_dtypes.bfloat16)
    kk = np.arange(128)[:, None]
    jj = np.arange(128)[None, :]
    c["mc"] = np.where(jj >= kk, 0.0, NEG).astype(np.float32)
    jj2 = np.arange(384)[None, :]
    c["ml"] = np.where((jj2 >= kk) & (jj2 < kk + WIN), 0.0, NEG).astype(np.float32)
    for name, base in (("g", 1_000_000.0), ("l", 10_000.0)):
        inv = 1.0 / base ** (np.arange(0, HD, 2, dtype=np.float64) / HD)
        fr = np.arange(S, dtype=np.float64)[:, None] * inv[None, :]
        cos = np.tile(np.cos(fr), (1, 16))
        sin = np.tile(np.sin(fr), (1, 16))
        c[f"rep_{name}"] = np.concatenate([cos, sin], axis=1).astype(np.float16)
    ind = np.zeros((8, 8 * 128), dtype=np.float32)
    for e in range(8):
        ind[e, 128 * e:128 * e + 128] = 1.0
    c["ind8"] = ind
    c["ones128_bf"] = np.ones((128, 1), dtype=ml_dtypes.bfloat16)
    c["ones1_f32"] = np.ones((1, 128), dtype=np.float32)
    return c


def _col_segs(j0, j1):
    segs = []
    for b0 in (0, 512):
        lo, hi = max(j0, b0), min(j1, b0 + 512)
        if lo < hi:
            segs.append((lo, hi))
    return segs


def build_nc(taps=(), stage=99):
    nc = bass.Bass("TRN2", target_bir_lowering=False, debug=False,
                   detect_race_conditions=False)
    cst = _host_constants()

    dx = nc.dram_tensor("x", [S, D], F32, kind="ExternalInput").ap()
    dn1 = nc.dram_tensor("n1w", [D, 1], F32, kind="ExternalInput").ap()
    dn2 = nc.dram_tensor("n2w", [D, 1], F32, kind="ExternalInput").ap()
    dgwq = nc.dram_tensor("gwq", [D, D], F32, kind="ExternalInput").ap()
    dgkvd = nc.dram_tensor("gkvd", [D, LAT], F32, kind="ExternalInput").ap()
    dgkup = nc.dram_tensor("gkup", [LAT, LAT], F32, kind="ExternalInput").ap()
    dgvup = nc.dram_tensor("gvup", [LAT, LAT], F32, kind="ExternalInput").ap()
    dgwo = nc.dram_tensor("gwo", [D, D], F32, kind="ExternalInput").ap()
    dlwq = nc.dram_tensor("lwq", [D, D], F32, kind="ExternalInput").ap()
    dlwk = nc.dram_tensor("lwk", [D, LAT], F32, kind="ExternalInput").ap()
    dlwv = nc.dram_tensor("lwv", [D, LAT], F32, kind="ExternalInput").ap()
    dlwo = nc.dram_tensor("lwo", [D, D], F32, kind="ExternalInput").ap()
    drwT = nc.dram_tensor("rwT", [D, E], F32, kind="ExternalInput").ap()
    dw1 = nc.dram_tensor("w1", [E, D, HF], F32, kind="ExternalInput").ap()
    dw2 = nc.dram_tensor("w2", [E, HF, D], F32, kind="ExternalInput").ap()
    dout = nc.dram_tensor("out", [S, D], F32, kind="ExternalOutput").ap()
    dw1b = nc.dram_tensor("w1b", [E, D, HF], BF16, kind="Internal").ap()
    dw2b = nc.dram_tensor("w2b", [E, HF, D], BF16, kind="Internal").ap()

    tapt = {}
    tapshape = dict(xn=([128, 16384], BF16), a1=([128, 2048], BF16),
                    qg=([128, 16384], BF16), kg0=([128, 2048], BF16),
                    kg1=([128, 2048], BF16), vg=([128, 4160], BF16),
                    gout=([128, 16384], BF16), h1=([128, 16384], BF16),
                    ql=([128, 16384], BF16), lout=([128, 16384], BF16),
                    s=([128, 16384], BF16), szm=([8, 2048], F32),
                    mkf=([8, 2048], BF16), rcpd=([128, 16], F32),
                    qps=([128, 1024], BF16), qro=([128, 1024], F32),
                    qn=([128, 1024], BF16), smallt=([128, 96], F32))
    for name in taps:
        sh, dt = tapshape[name]
        tapt[name] = nc.dram_tensor("tap_" + name, sh, dt,
                                    kind="ExternalOutput").ap()

    c_ident = nc.inline_tensor(np.asarray(cst["ident"]), "c_ident")
    c_mc = nc.inline_tensor(cst["mc"], "c_mc")
    c_ml = nc.inline_tensor(cst["ml"], "c_ml")
    c_repg = nc.inline_tensor(np.asarray(cst["rep_g"]), "c_repg")
    c_repl = nc.inline_tensor(np.asarray(cst["rep_l"]), "c_repl")
    c_ind8 = nc.inline_tensor(cst["ind8"], "c_ind8")
    c_o128 = nc.inline_tensor(np.asarray(cst["ones128_bf"]), "c_o128")
    c_o1 = nc.inline_tensor(cst["ones1_f32"], "c_o1")

    with ExitStack() as stk:
        en = stk.enter_context
        arena = nc.alloc_sbuf_tensor("arena", [128, ARENA], mybir.dt.uint8)
        abase = nc.lookup_mloc(arena).addr

        def at(name, off, shape, dtype):
            return nc.alloc_sbuf_tensor_at(name, shape, dtype, offset=abase + off)

        XN = at("XN", OFF_XN, [128, 16384], BF16)
        BA = at("BA", OFF_A, [128, 16384], BF16)
        BB = at("BB", OFF_B, [128, 16384], BF16)
        BC = at("BC", OFF_C, [128, 16384], BF16)
        BW = at("BW", OFF_W, [128, 16384], BF16)

        K0 = at("K0", OFF_K, [128, 2048], BF16)
        K1 = at("K1", OFF_K + 4096, [128, 2048], BF16)
        VA = at("VA", OFF_K + 8192, [128, 4160], BF16)
        ROPE = at("ROPE", OFF_K + 16512, [128, 2048], FP16)
        QSQ = at("QSQ", OFF_K + 20608, [128, 1024], BF16)
        QRO = at("QRO", OFF_K + 22656, [128, 1024], F32)
        QN = at("QN", OFF_K + 26752, [128, 1024], BF16)
        LATS = at("LATS", OFF_K + 28800, [128, 512], BF16)
        KSQ = at("KSQ", OFF_K + 29824, [128, 256], BF16)
        KRO = at("KRO", OFF_K + 30336, [128, 256], F32)
        KN = at("KN", OFF_K + 31360, [128, 256], BF16)
        PROBS = at("PROBS", OFF_K + 20608, [128, 2048], BF16)
        MSCR = at("MSCR", OFF_K + 24704, [128, 384], F32)
        RBCS = at("RBCS", OFF_K + 26240, [128, 512], F32)
        RDR = at("RDR", OFF_K + 28288, [1, 1024], F32)
        SRZ = at("SRZ", OFF_K, [8, 512], F32)
        SZM = at("SZM", OFF_K + 2048, [8, 2048], F32)
        MKF = at("MKF", OFF_K + 10240, [8, 2048], BF16)
        WBCS = at("WBCS", OFF_K + 14336, [128, 2048], F32)

        XST = at("XST", OFF_T, [128, 2048], BF16)
        ATILE = at("ATILE", OFF_T, [128, 2048], BF16)
        GSB = at("GSB", OFF_T, [128, 1024], F32)
        OUTB = at("OUTB", OFF_T, [128, 1024], F32)
        LGT = at("LGT", OFF_T, [128, 48], F32)
        MKT = at("MKT", OFF_T + 192, [128, 8], BF16)
        ZTM = at("ZTM", OFF_T + 224, [128, 8], F32)
        XSQ = at("XSQ", OFF_T + 4096, [128, 512], BF16)
        OUTA = at("OUTA", OFF_T + 4096, [128, 1024], F32)
        XNT = at("XNT", OFF_T + 5120, [128, 256], F32)
        ROWS = at("ROWS", OFF_T + 6144, [1, 768], F32)
        PRR = at("PRR", OFF_T + 6144, [1, 512], F32)
        SMALLT = at("SMALLT", OFF_T + 5120, [128, 96], F32)
        RSC1 = at("RSC1", OFF_T + 8192, [128, 512], F32)
        RSC2 = at("RSC2", OFF_T + 10240, [128, 512], F32)
        XRL = at("XRL", OFF_T + 8192, [128, 1024], F32)
        IND8 = at("IND8", OFF_K + 22528, [8, 1024], F32)
        HBF = at("HBF", OFF_K + 26624, [128, 1024], BF16)
        ASCR = at("ASCR", OFF_K + 28672, [128, 512], F32)

        N1W = nc.alloc_sbuf_tensor("N1W", [128, 8], F32)
        N2W = nc.alloc_sbuf_tensor("N2W", [128, 8], F32)
        RWT = nc.alloc_sbuf_tensor("RWT", [128, 64], BF16)
        RCPD = nc.alloc_sbuf_tensor("RCPD", [128, 16], F32)
        CC = nc.alloc_sbuf_tensor("CC", [128, 8], F32)
        CM05 = nc.alloc_sbuf_tensor("CM05", [128, 1], F32)
        IDENT = nc.alloc_sbuf_tensor("IDENT", [128, 128], BF16)
        MCT = nc.alloc_sbuf_tensor("MCT", [128, 128], F32)
        MLT = nc.alloc_sbuf_tensor("MLT", [128, 384], F32)
        O128 = nc.alloc_sbuf_tensor("O128", [128, 1], BF16)
        ZCOL = nc.alloc_sbuf_tensor("ZCOL", [128, 65], BF16)
        O1 = nc.alloc_sbuf_tensor("O1", [1, 128], F32)
        assert nc.sbuf_bytes_remaining >= 0, nc.sbuf_bytes_remaining

        PSA = en(nc.psum_tensor("PSA", [128, 1024], F32))
        PSB = en(nc.psum_tensor("PSB", [128, 1024], F32))
        PSC = en(nc.psum_tensor("PSC", [128, 1024], F32))
        PSD = en(nc.psum_tensor("PSD", [128, 512], F32))
        PST = en(nc.psum_tensor("PST", [128, 1024], BF16))

        prog = []
        counts = defaultdict(int)
        guards = {}

        def emit(engine, fn, waits=(), inc=None, ninc=1, reads=(), writes=()):
            wl = list(waits)
            for slot in writes:
                g = guards.get(slot)
                if g is not None:
                    wl.append(g)
            wl = [(s, v) for (s, v) in wl if v > 0]
            dd = {}
            for s, v in wl:
                dd[s] = max(dd.get(s, 0), v)
            prog.append((engine, fn, sorted(dd.items()), inc, ninc))
            if inc is not None:
                counts[inc] += ninc
                for slot in reads:
                    guards[slot] = (inc, counts[inc])

        def W(s, v=None):
            return (s, counts[s] if v is None else v)

        def fc(c, t0):
            return 2048 * c + t0

        def phase_guard(name):
            snap = [(s, c) for s, c in counts.items()
                    if c > 0 and not s.startswith("d_")]
            for eng in ("tensor", "vector", "scalar"):
                emit(eng, lambda e: e.nop(), waits=snap)

        for dst, src in ((IDENT, c_ident), (MCT, c_mc), (MLT, c_ml),
                         (O128, c_o128), (O1, c_o1)):
            emit("sync", lambda eng, d=dst, s=src: eng.dma_start(d[:], s.ap()),
                 inc="d_tab", ninc=16)
        emit("sync", lambda eng: eng.dma_start(
            N1W[:].rearrange("p (c one) -> p c one", one=1),
            dn1.rearrange("(c p) one -> p c one", p=128),
            allow_slow_non_contiguous=True), inc="d_tab", ninc=16)
        emit("sync", lambda eng: eng.dma_start(
            N2W[:].rearrange("p (c one) -> p c one", one=1),
            dn2.rearrange("(c p) one -> p c one", p=128),
            allow_slow_non_contiguous=True), inc="d_tab", ninc=16)
        emit("gpsimd", lambda eng: eng.dma_start(
            RWT[:].rearrange("p (c f) -> p c f", f=8),
            drwT.rearrange("(c p) f -> p c f", p=128)), inc="d_rwt", ninc=16)

        def gen_consts(eng):
            eng.memset(CM05[:], -0.5)
            eng.memset(ZCOL[:], 0.0)
            for col, v in ((0, 1e-5), (1, 1e-6), (2, 1.0 / 1024), (3, 1.0 / 64),
                           (4, 0.125), (5, 0.5), (6, 1.0), (7, -1.0)):
                eng.memset(CC[:, col:col + 1], v)
            last = None
            for T in range(16):
                for g in range(4):
                    o = 260 * T + 65 * g + 64
                    last = eng.memset(VA[:, o:o + 1], 1.0)
            return last
        emit("vector", gen_consts, waits=[W("d_tab")], inc="v_cst")

        def wload(dst_c0, w_ap, kdim, ndim, sem="d_w", waits=()):
            nch = kdim // 128
            per = max(1, 2048 // ndim)
            first = True
            for c0 in range(0, nch, per):
                c1 = min(nch, c0 + per)
                def f(eng, dst_c0=dst_c0, w_ap=w_ap, ndim=ndim, c0=c0, c1=c1):
                    return eng.dma_start(
                        BW[:, dst_c0 + c0 * ndim:dst_c0 + c1 * ndim]
                          .rearrange("p (c n) -> p c n", n=ndim),
                        w_ap[128 * c0:128 * c1, :]
                        .rearrange("(c p) n -> p c n", p=128))
                emit("gpsimd", f, waits=waits if first else (), inc=sem, ninc=16)
                first = False

        wload(0, dgwq, D, D)
        wload(8192, dgkvd, D, LAT)
        wload(10240, dgkup, LAT, LAT)
        wload(10752, dgvup, LAT, LAT)
        WLD = {"g": counts["d_w"]}

        for t in range(16):
            sl = (t % 2) * 1024
            wt = [W("pe_xtp", t - 1)] if t >= 2 else []
            emit("gpsimd", lambda eng, t=t, sl=sl: eng.dma_start(
                XST[:, sl:sl + 1024], dx[128 * t:128 * t + 128, :]),
                waits=wt, inc="d_x%d" % (t % 2), ninc=16)
        for e in range(E):
            for k4 in range(4):
                emit("gpsimd", lambda eng, e=e, k4=k4: eng.dma_start(
                    dw1b[e, 256 * k4:256 * k4 + 256, :],
                    dw1[e, 256 * k4:256 * k4 + 256, :]), inc="d_conv", ninc=16)
                emit("gpsimd", lambda eng, e=e, k4=k4: eng.dma_start(
                    dw2b[e, 512 * k4:512 * k4 + 512, :],
                    dw2[e, 512 * k4:512 * k4 + 512, :]), inc="d_conv", ninc=16)
        for t in range(16):
            sl = (t % 2) * 1024
            def ftp(eng, sl=sl):
                for c in range(NCH):
                    i = eng.transpose(PST[:, 128 * c:128 * c + 128],
                                      XST[:, sl + 128 * c:sl + 128 * c + 128],
                                      IDENT[:])
                return i
            emit("tensor", ftp,
                 waits=[W("d_tab", 112), W("d_x%d" % (t % 2), 16 * (t // 2 + 1))],
                 reads=("xst%d" % (t % 2),), writes=("pst",), inc="pe_xtp")
            def fev(eng, t=t):
                for c in range(NCH):
                    i = eng.tensor_copy(BA[:, fc(c, 128 * t):fc(c, 128 * t) + 128],
                                        PST[:, 128 * c:128 * c + 128])
                return i
            emit("vector", fev, waits=[W("pe_xtp")], reads=("pst",), inc="v_xfm")

        r0, r1, r2 = ROWS[0:1, 0:256], ROWS[0:1, 256:512], ROWS[0:1, 512:768]
        for sp in range(8):
            t0 = 256 * sp
            xfm_done = W("v_xfm", 2 * (sp + 1))
            def fsum(eng, t0=t0):
                for c in range(NCH):
                    i = eng.matmul(PSD[0:1, 0:256], O128[:],
                                   BA[:, fc(c, t0):fc(c, t0) + 256],
                                   start=(c == 0), stop=(c == NCH - 1))
                return i
            emit("tensor", fsum, waits=[xfm_done], writes=("psd",), inc="pe_lnsum")
            for c in range(NCH):
                sl = (c % 2) * 256
                def fsq(eng, c=c, sl=sl, t0=t0):
                    src = BA[:, fc(c, t0):fc(c, t0) + 256]
                    return eng.tensor_tensor(XSQ[:, sl:sl + 256], src, src, ALU.mult)
                emit("gpsimd", fsq, waits=[xfm_done],
                     reads=(), writes=("xsq%d" % (c % 2),), inc="po_sq")
                def fss(eng, c=c):
                    sl = (c % 2) * 256
                    return eng.matmul(PSC[0:1, 0:256], O128[:], XSQ[:, sl:sl + 256],
                                      start=(c == 0), stop=(c == NCH - 1))
                emit("tensor", fss, waits=[W("po_sq", 8 * sp + c + 1)],
                     reads=("xsq%d" % (c % 2),),
                     writes=("psc",) if c == 0 else (), inc="pe_lnss")
            emit("vector", lambda eng: eng.tensor_scalar(
                r0, PSD[0:1, 0:256], CC[0:1, 2:3], None, ALU.mult),
                waits=[W("pe_lnsum"), W("v_cst", 1)], reads=("psd",),
                inc="v_lnm")
            emit("vector", lambda eng: eng.tensor_tensor(r1, r0, r0, ALU.mult),
                 waits=[W("v_lnm")], inc="v_lnmsq")
            emit("vector", lambda eng: eng.scalar_tensor_tensor(
                r2, PSC[0:1, 0:256], CC[0:1, 2:3], r1, ALU.mult, ALU.subtract),
                waits=[W("pe_lnss"), W("v_lnmsq")], reads=("psc",),
                inc="v_lnvar")
            emit("scalar", lambda eng: eng.activation(
                r1, r2, AF.Ln, bias=CC[0:1, 0:1], scale=1.0),
                waits=[W("v_lnvar")], inc="sc_ln1")
            emit("scalar", lambda eng: eng.activation(
                r2, r1, AF.Exp, bias=0.0, scale=CM05[0:1, :]),
                waits=[W("sc_ln1")], inc="sc_lnsd")
            emit("vector", lambda eng: eng.tensor_tensor(r1, r0, r2, ALU.mult),
                 waits=[W("sc_lnsd")], inc="v_lnrow")
            def fbc(eng):
                eng.matmul(PSA[:, 0:256], O1[:], r2, start=True, stop=True)
                return eng.matmul(PSA[:, 512:768], O1[:], r1, start=True, stop=True)
            emit("tensor", fbc, waits=[W("v_lnrow")], writes=("psa",), inc="pe_lnbc")
            for c in range(NCH):
                def fxn(eng, c=c, t0=t0):
                    src = BA[:, fc(c, t0):fc(c, t0) + 256]
                    eng.tensor_tensor(XNT[:], src, PSA[:, 0:256], ALU.mult)
                    return eng.tensor_tensor(XN[:, fc(c, t0):fc(c, t0) + 256],
                                             XNT[:], PSA[:, 512:768], ALU.subtract)
                emit("vector", fxn, waits=[W("pe_lnbc")],
                     reads=("psa",) if c == NCH - 1 else (), inc="v_xn")

        def gen_prep(phase):
            srcbuf = XN if phase == "g" else BA
            wcol = N1W if phase == "g" else N2W
            qdst = BA if phase == "g" else BB
            ro = 0 if phase == "g" else 1024
            sdrow, rrow = PRR[0:1, 256:512], PRR[0:1, 0:256]
            for pt in range(8):
                t256 = 256 * pt
                srcrdy = (W("v_xn", 8 * (pt + 1)) if phase == "g"
                          else W("v_h1", 8 * (pt // 2 + 1)))
                for c in range(NCH):
                    sl = (c % 2) * 256
                    def fsq(eng, c=c, sl=sl, t256=t256, srcbuf=srcbuf):
                        src = srcbuf[:, fc(c, t256):fc(c, t256) + 256]
                        return eng.tensor_tensor(XSQ[:, sl:sl + 256], src, src,
                                                 ALU.mult)
                    emit("gpsimd", fsq, waits=[srcrdy],
                         writes=("xsq%d" % (c % 2),), inc="po_psq")
                    def fss(eng, c=c):
                        sl = (c % 2) * 256
                        return eng.matmul(PSD[0:1, 0:256], O128[:], XSQ[:, sl:sl + 256],
                                          start=(c == 0), stop=(c == NCH - 1))
                    emit("tensor", fss, waits=[W("po_psq")],
                         reads=("xsq%d" % (c % 2),),
                         writes=("psd",) if c == 0 else (), inc="pe_pss")
                emit("scalar", lambda eng, sdrow=sdrow: eng.activation(
                    sdrow, PSD[0:1, 0:256], AF.Ln,
                    bias=CC[0:1, 1:2], scale=CC[0:1, 2:3]),
                    waits=[W("pe_pss")], reads=("psd",), inc="sc_psd0")
                emit("scalar", lambda eng, sdrow=sdrow, rrow=rrow: eng.activation(
                    rrow, sdrow, AF.Exp, bias=0.0, scale=CM05[0:1, :]),
                    waits=[W("sc_psd0")], inc="sc_psd")
                emit("tensor", lambda eng, rrow=rrow: eng.matmul(
                    PSC[:, 0:256], O1[:], rrow, start=True, stop=True),
                    waits=[W("sc_psd")], writes=("psc",), inc="pe_pbc")
                for c in range(NCH):
                    def fa(eng, c=c, t256=t256, srcbuf=srcbuf, wcol=wcol):
                        return eng.scalar_tensor_tensor(
                            ATILE[:, 256 * c:256 * c + 256],
                            srcbuf[:, fc(c, t256):fc(c, t256) + 256],
                            wcol[:, c:c + 1], PSC[:, 0:256], ALU.mult, ALU.mult)
                    emit("vector", fa, waits=[W("pe_pbc")],
                         reads=("psc",) if c == NCH - 1 else (),
                         writes=("atile",), inc="v_atile")
                atile_rdy = W("v_atile")
                if PREP_SUB < 1:
                    continue
                if phase == "g":
                    def flat(eng):
                        i = None
                        for lc in range(2):
                            for c in range(NCH):
                                i = eng.matmul(
                                    PSB[:, 512 * lc:512 * lc + 256],
                                    BW[:, 8192 + 256 * c + 128 * lc:
                                       8192 + 256 * c + 128 * lc + 128],
                                    ATILE[:, 256 * c:256 * c + 256],
                                    start=(c == 0), stop=(c == NCH - 1))
                        return i
                    emit("tensor", flat,
                         waits=[atile_rdy, W("d_w", WLD["g"])],
                         reads=("atile",), writes=("psb",), inc="pe_lat")
                    if PREP_SUB >= 1.5:
                        def flatev(eng):
                            eng.tensor_copy(LATS[:, 0:256], PSB[:, 0:256])
                            return eng.tensor_copy(LATS[:, 256:512], PSB[:, 512:768])
                        emit("vector", flatev, waits=[W("pe_lat")],
                             reads=("psb",), writes=("lats",), inc="v_latev")
                for j in range(2):
                    if PREP_SUB < 2:
                        continue
                    T = 2 * pt + j
                    qp, qslot = (PSA, "psa") if j == 0 else (PSB, "psb")
                    def fq(eng, j=j, qp=qp):
                        i = None
                        for c in range(NCH):
                            for u in range(2):
                                i = eng.matmul(
                                    qp[:, 512 * u:512 * u + 512],
                                    ATILE[:, 256 * c + 128 * j:256 * c + 128 * j + 128],
                                    BW[:, 1024 * c + 512 * u:1024 * c + 512 * u + 512],
                                    start=(c == 0), stop=(c == NCH - 1))
                        return i
                    qw = [atile_rdy,
                          (W("d_w", WLD["g"]) if phase == "g" else W("d_wlq", 64))]
                    emit("tensor", fq,
                         waits=qw, reads=("atile",), writes=(qslot,), inc="pe_q")
                    rsl = 1024 * (T % 2)
                    rsem = "d_rope%d" % (T % 2)
                    def frld(eng, T=T, rsl=rsl, phase=phase):
                        return eng.dma_start(
                            ROPE[:, rsl:rsl + 1024],
                            (c_repg if phase == "g" else c_repl)
                            .ap()[128 * T:128 * T + 128, :])
                    emit("sync", frld, writes=("rope%d" % (T % 2),),
                         inc=rsem, ninc=16)
                    rld = counts[rsem]
                    def fqro(eng, qp=qp, rsl=rsl):
                        cosS = ROPE[:, rsl:rsl + 512]
                        sinS = ROPE[:, rsl + 512:rsl + 1024]
                        e, o = qp[:, 0:512], qp[:, 512:1024]
                        eng.tensor_tensor(RSC1[:], e, cosS, ALU.mult)
                        eng.tensor_tensor(RSC2[:], o, sinS, ALU.mult)
                        eng.tensor_tensor(QRO[:, 0:512], RSC1[:], RSC2[:],
                                          ALU.subtract)
                        eng.tensor_tensor(RSC1[:], e, sinS, ALU.mult)
                        eng.tensor_tensor(RSC2[:], o, cosS, ALU.mult)
                        return eng.tensor_tensor(QRO[:, 512:1024], RSC1[:], RSC2[:],
                                                 ALU.add)
                    emit("vector", fqro, waits=[W("pe_q"), (rsem, rld)],
                         reads=(qslot, "rope%d" % (T % 2)), writes=("qro",),
                         inc="v_qro2")
                    emit("gpsimd", lambda eng: eng.tensor_tensor(
                        QSQ[:], QRO[:], QRO[:], ALU.mult),
                        waits=[W("v_qro2")], reads=("qro",), writes=("qsq",),
                        inc="po_qsq")
                    def fqss(eng):
                        i = None
                        qv = QSQ.ap().rearrange("p (half rest) -> p half rest",
                                                half=2)
                        for h in range(H):
                            i = eng.reduce_sum(SMALLT[:, 64 + h:65 + h],
                                               qv[:, :, 32 * h:32 * h + 32],
                                               axis=mybir.AxisListType.XY)
                        return i
                    emit("vector", fqss, waits=[W("po_qsq")], reads=("qsq",),
                         writes=("smallt",), inc="v_qss")
                    emit("scalar", lambda eng: eng.activation(
                        SMALLT[:, 80:96], SMALLT[:, 64:80], AF.Sqrt,
                        bias=CC[:, 1:2], scale=CC[:, 3:4]),
                        waits=[W("v_qss")], reads=("smallt",), inc="sc_qsd")
                    emit("vector", lambda eng: eng.reciprocal(
                        SMALLT[:, 64:80], SMALLT[:, 80:96]),
                        waits=[W("sc_qsd")], inc="v_qrc")
                    def fqn(eng):
                        i = None
                        for h in range(H):
                            cn = 128 * ((h % 4) + 4 * (h // 8)) + 64 * ((h // 4) % 2)
                            i = eng.tensor_scalar(QN[:, cn:cn + 32],
                                                  QRO[:, 32 * h:32 * h + 32],
                                                  SMALLT[:, 64 + h:65 + h], None,
                                                  ALU.mult)
                            i = eng.tensor_scalar(QN[:, cn + 32:cn + 64],
                                                  QRO[:, 512 + 32 * h:512 + 32 * h + 32],
                                                  SMALLT[:, 64 + h:65 + h], None,
                                                  ALU.mult)
                        return i
                    emit("vector", fqn, waits=[W("v_qrc")], reads=("qro",),
                         writes=("qn",), inc="v_qn")
                    def fqtp(eng):
                        i = None
                        for c in range(NCH):
                            i = eng.transpose(PST[:, 128 * c:128 * c + 128],
                                              QN[:, 128 * c:128 * c + 128], IDENT[:])
                        return i
                    emit("tensor", fqtp, waits=[W("v_qn")],
                         reads=("qn",), writes=("pst",), inc="pe_ptp")
                    def fqev(eng, T=T, qdst=qdst):
                        i = None
                        for c in range(NCH):
                            i = eng.tensor_copy(
                                qdst[:, fc(c, 128 * T):fc(c, 128 * T) + 128],
                                PST[:, 128 * c:128 * c + 128])
                        return i
                    emit("vector", fqev, waits=[W("pe_ptp")], reads=("pst",),
                         inc="v_ptev")
                    if PREP_SUB < 3:
                        continue
                    def fk(eng, j=j, phase=phase):
                        i = None
                        if phase == "g":
                            for lc in range(2):
                                i = eng.matmul(
                                    PSD[:, 0:256],
                                    LATS[:, 256 * lc + 128 * j:256 * lc + 128 * j + 128],
                                    BW[:, 10240 + 256 * lc:10240 + 256 * lc + 256],
                                    start=(lc == 0), stop=(lc == 1))
                        else:
                            for c in range(NCH):
                                i = eng.matmul(
                                    PSD[:, 0:256],
                                    ATILE[:, 256 * c + 128 * j:256 * c + 128 * j + 128],
                                    BW[:, 8192 + 256 * c:8192 + 256 * c + 256],
                                    start=(c == 0), stop=(c == NCH - 1))
                        return i
                    kwaits = ([W("v_latev")] if phase == "g"
                              else [atile_rdy, W("d_wlk", 16)])
                    emit("tensor", fk, waits=kwaits,
                         reads=("lats",) if phase == "g" else ("atile",),
                         writes=("psd",), inc="pe_k")
                    def fkro(eng, rsl=rsl):
                        cosS = ROPE[:, rsl:rsl + 128]
                        sinS = ROPE[:, rsl + 512:rsl + 640]
                        e, o = PSD[:, 0:128], PSD[:, 128:256]
                        eng.tensor_tensor(RSC1[:, 0:128], e, cosS, ALU.mult)
                        eng.tensor_tensor(RSC2[:, 0:128], o, sinS, ALU.mult)
                        eng.tensor_tensor(KRO[:, 0:128], RSC1[:, 0:128],
                                          RSC2[:, 0:128], ALU.subtract)
                        eng.tensor_tensor(RSC1[:, 0:128], e, sinS, ALU.mult)
                        eng.tensor_tensor(RSC2[:, 0:128], o, cosS, ALU.mult)
                        return eng.tensor_tensor(KRO[:, 128:256], RSC1[:, 0:128],
                                                 RSC2[:, 0:128], ALU.add)
                    emit("vector", fkro, waits=[W("pe_k"), (rsem, rld)],
                         reads=("psd", "rope%d" % (T % 2)), writes=("kro",),
                         inc="v_kro2")
                    emit("gpsimd", lambda eng: eng.tensor_tensor(
                        KSQ[:], KRO[:], KRO[:], ALU.mult),
                        waits=[W("v_kro2")], reads=("kro",), writes=("ksq",),
                        inc="po_ksq")
                    def fkss(eng):
                        i = None
                        kv2 = KSQ.ap().rearrange("p (half rest) -> p half rest",
                                                 half=2)
                        for g in range(KVH):
                            i = eng.reduce_sum(SMALLT[:, 64 + g:65 + g],
                                               kv2[:, :, 32 * g:32 * g + 32],
                                               axis=mybir.AxisListType.XY)
                        return i
                    emit("vector", fkss, waits=[W("po_ksq")], reads=("ksq",),
                         writes=("smallt",), inc="v_kss")
                    emit("scalar", lambda eng: eng.activation(
                        SMALLT[:, 80:84], SMALLT[:, 64:68], AF.Sqrt,
                        bias=CC[:, 1:2], scale=CC[:, 3:4]),
                        waits=[W("v_kss")], reads=("smallt",), inc="sc_ksd")
                    emit("vector", lambda eng: eng.reciprocal(
                        SMALLT[:, 64:68], SMALLT[:, 80:84]),
                        waits=[W("sc_ksd")], inc="v_krc")
                    def fkn(eng):
                        i = None
                        for g in range(KVH):
                            i = eng.tensor_scalar(KN[:, 64 * g:64 * g + 32],
                                                  KRO[:, 32 * g:32 * g + 32],
                                                  SMALLT[:, 64 + g:65 + g], None,
                                                  ALU.mult)
                            i = eng.tensor_scalar(KN[:, 64 * g + 32:64 * g + 64],
                                                  KRO[:, 128 + 32 * g:128 + 32 * g + 32],
                                                  SMALLT[:, 64 + g:65 + g], None,
                                                  ALU.mult)
                        return i
                    emit("vector", fkn, waits=[W("v_krc")], reads=("kro",),
                         writes=("kn",), inc="v_kn")
                    def fktp(eng):
                        eng.transpose(PST[:, 0:128], KN[:, 0:128], IDENT[:])
                        return eng.transpose(PST[:, 128:256], KN[:, 128:256], IDENT[:])
                    emit("tensor", fktp, waits=[W("v_kn")],
                         reads=("kn",), writes=("pst",), inc="pe_ptp")
                    def fkev(eng, T=T):
                        eng.tensor_copy(K0[:, 128 * T:128 * T + 128], PST[:, 0:128])
                        return eng.tensor_copy(K1[:, 128 * T:128 * T + 128],
                                               PST[:, 128:256])
                    emit("vector", fkev, waits=[W("pe_ptp")], reads=("pst",),
                         inc="v_ptev")
                    if PREP_SUB < 4:
                        continue
                    def fv(eng, j=j, phase=phase):
                        i = None
                        if phase == "g":
                            for lc in range(2):
                                i = eng.matmul(
                                    PSD[:, 256:512],
                                    LATS[:, 256 * lc + 128 * j:256 * lc + 128 * j + 128],
                                    BW[:, 10752 + 256 * lc:10752 + 256 * lc + 256],
                                    start=(lc == 0), stop=(lc == 1))
                        else:
                            for c in range(NCH):
                                i = eng.matmul(
                                    PSD[:, 256:512],
                                    ATILE[:, 256 * c + 128 * j:256 * c + 128 * j + 128],
                                    BW[:, 11264 + 256 * c:11264 + 256 * c + 256],
                                    start=(c == 0), stop=(c == NCH - 1))
                        return i
                    vwaits = [W("v_kro2")]
                    vwaits.append(W("v_latev") if phase == "g" else atile_rdy)
                    if phase == "l":
                        vwaits.append(W("d_wlv", 16))
                    emit("tensor", fv, waits=vwaits,
                         reads=("lats",) if phase == "g" else ("atile",),
                         writes=("psd2",), inc="pe_v")
                    def fvev(eng, T=T):
                        i = None
                        for g in range(KVH):
                            i = eng.tensor_copy(
                                VA[:, 260 * T + 65 * g:260 * T + 65 * g + 64],
                                PSD[:, 256 + 64 * g:256 + 64 * g + 64])
                        return i
                    emit("vector", fvev, waits=[W("pe_v"), W("v_cst", 1)],
                         reads=("psd2", "psd"), inc="v_vev")
                    guards["psd"] = ("v_vev", counts["v_vev"])

        def gen_attn(phase):
            qbuf = BA if phase == "g" else BB
            obuf = BB if phase == "g" else BC
            for h in range(H):
                kv = h // 4
                ktens = K0 if kv < 2 else K1
                krow = 64 * (kv % 2)
                qch = (h % 4) + 4 * (h // 8)
                qrow = 64 * ((h // 4) % 2)
                for qb in range(2):
                    qlo = 1024 * qb
                    steps = []
                    for kb in range(S // 128):
                        k0 = 128 * kb
                        if phase == "g":
                            lo = max(qlo, k0)
                            if lo >= qlo + 1024:
                                continue
                            j0, j1 = lo - qlo, 1024
                            if k0 >= qlo:
                                jm0, jm1 = j0, min(j0 + 128, j1)
                            else:
                                jm0 = jm1 = j0
                        else:
                            lo, hi = max(qlo, k0), min(qlo + 1024, k0 + 128 + WIN)
                            if lo >= hi:
                                continue
                            j0, j1 = lo - qlo, hi - qlo
                            jm0, jm1 = j0, j1
                        steps.append((kb, j0, j1, jm0, jm1))
                    nst = len(steps)
                    if phase == "l":
                        def fzero(eng):
                            eng.matmul(PSC[0:65, 0:512], ZCOL[:], K0[:, 0:512],
                                       start=True, stop=False)
                            return eng.matmul(PSC[0:65, 512:1024], ZCOL[:],
                                              K0[:, 0:512], start=True, stop=False)
                        emit("tensor", fzero, waits=[W("v_cst", 1)],
                             writes=("psc",), inc="pe_pvz")
                    def mk_fsc(kb, j0, j1, scp):
                        def fsc(eng, kb=kb, j0=j0, j1=j1, scp=scp, ktens=ktens,
                                krow=krow, qch=qch, qrow=qrow, qlo=qlo, qbuf=qbuf):
                            i = None
                            for (a, b) in _col_segs(j0, j1):
                                i = eng.matmul(
                                    scp[:, a:b],
                                    ktens[krow:krow + 64, 128 * kb:128 * kb + 128],
                                    qbuf[qrow:qrow + 64,
                                         fc(qch, qlo + a):fc(qch, qlo + a) + (b - a)],
                                    start=True, stop=True)
                            return i
                        return fsc
                    sccnt = {}
                    def emit_sc(si):
                        kb, j0, j1, _, _ = steps[si]
                        scslot = "psa" if kb % 2 == 0 else "psb"
                        scp = PSA if kb % 2 == 0 else PSB
                        emit("tensor", mk_fsc(kb, j0, j1, scp), writes=(scslot,),
                             inc="pe_sc")
                        sccnt[si] = counts["pe_sc"]
                    emit_sc(0)
                    for si, (kb, j0, j1, jm0, jm1) in enumerate(steps):
                        scp, scslot = (PSA, "psa") if kb % 2 == 0 else (PSB, "psb")
                        pp0 = (kb % 2) * 1024
                        ppslot = "probs%d" % (kb % 2)
                        if si + 1 < len(steps):
                            emit_sc(si + 1)
                        if jm1 > jm0 and jm1 < j1:
                            mt = MCT if phase == "g" else MLT
                            mo = qlo + jm0 - 128 * kb
                            def fmask(eng, jm0=jm0, jm1=jm1, scp=scp, mt=mt, mo=mo):
                                eng.tensor_tensor(
                                    MSCR[:, 0:jm1 - jm0], scp[:, jm0:jm1],
                                    mt[:, mo:mo + jm1 - jm0], ALU.add)
                                return eng.tensor_copy(scp[:, jm0:jm1],
                                                       MSCR[:, 0:jm1 - jm0])
                            emit("vector", fmask,
                                 waits=[W("pe_sc", sccnt[si]), W("d_tab", 112)],
                                 writes=("mscr",), inc="v_msk")
                            def fexp(eng, j0=j0, j1=j1, scp=scp, pp0=pp0):
                                return eng.activation(
                                    PROBS[:, pp0 + j0:pp0 + j1], scp[:, j0:j1],
                                    AF.Exp, bias=0.0, scale=CC[:, 4:5])
                            emit("scalar", fexp,
                                 waits=[W("pe_sc", sccnt[si]), W("v_msk")],
                                 reads=(scslot, "mscr"), writes=(ppslot,),
                                 inc="sc_exp")
                        elif jm1 > jm0:
                            mt = MCT if phase == "g" else MLT
                            mo = qlo + jm0 - 128 * kb
                            def fmask(eng, jm0=jm0, jm1=jm1, scp=scp, mt=mt, mo=mo):
                                return eng.tensor_tensor(
                                    MSCR[:, 0:jm1 - jm0], scp[:, jm0:jm1],
                                    mt[:, mo:mo + jm1 - jm0], ALU.add)
                            emit("vector", fmask,
                                 waits=[W("pe_sc", sccnt[si]), W("d_tab", 112)],
                                 writes=("mscr",), inc="v_msk")
                            def fexp(eng, jm0=jm0, jm1=jm1, pp0=pp0):
                                return eng.activation(
                                    PROBS[:, pp0 + jm0:pp0 + jm1],
                                    MSCR[:, 0:jm1 - jm0], AF.Exp,
                                    bias=0.0, scale=CC[:, 4:5])
                            emit("scalar", fexp, waits=[W("v_msk")],
                                 reads=(scslot, "mscr"), writes=(ppslot,),
                                 inc="sc_exp")
                        else:
                            def fexp(eng, j0=j0, j1=j1, scp=scp, pp0=pp0):
                                return eng.activation(
                                    PROBS[:, pp0 + j0:pp0 + j1], scp[:, j0:j1],
                                    AF.Exp, bias=0.0, scale=CC[:, 4:5])
                            emit("scalar", fexp, waits=[W("pe_sc", sccnt[si])],
                                 reads=(scslot,), writes=(ppslot,), inc="sc_exp")
                        def fpv(eng, kb=kb, si=si, steps=tuple(steps), pp0=pp0,
                                kv=kv):
                            i = None
                            for (a, b) in _col_segs(steps[si][1], steps[si][2]):
                                bank = 0 if a < 512 else 1
                                first = True
                                for s2 in steps[:si]:
                                    if max(s2[1], 512 * bank) < min(s2[2],
                                                                    512 * bank + 512):
                                        first = False
                                        break
                                last = True
                                for s2 in steps[si + 1:]:
                                    if max(s2[1], 512 * bank) < min(s2[2],
                                                                    512 * bank + 512):
                                        last = False
                                        break
                                i = eng.matmul(
                                    PSC[0:65, a:b],
                                    VA[:, 260 * kb + 65 * kv:260 * kb + 65 * kv + 65],
                                    PROBS[:, pp0 + a:pp0 + b],
                                    start=(first and phase == "g"), stop=last)
                            return i
                        emit("tensor", fpv, waits=[W("sc_exp")],
                             reads=(ppslot,),
                             writes=("psc",) if si == 0 else (), inc="pe_pv")
                    emit("scalar", lambda eng: eng.activation(
                        RDR[:], PSC[64:65, :], AF.Ln, bias=0.0, scale=1.0),
                        waits=[W("pe_pv")], writes=("rdr",), inc="v_rd")
                    for half in range(2):
                        def fbcd(eng, half=half):
                            return eng.matmul(PSD[:, 0:512], O1[:],
                                              RDR[0:1, 512 * half:512 * half + 512],
                                              start=True, stop=True)
                        emit("tensor", fbcd, waits=[W("v_rd")],
                             reads=("rdr",) if half == 1 else (),
                             writes=("psd",), inc="pe_rbc")
                        emit("scalar", lambda eng: eng.activation(
                            RBCS[:], PSD[:, 0:512], AF.Exp, bias=0.0,
                            scale=CC[:, 7:8]),
                            waits=[W("pe_rbc")], reads=("psd",), writes=("rbcs",),
                            inc="sc_rbc")
                        def faev(eng, half=half, qrow=qrow, qch=qch, qlo=qlo,
                                 obuf=obuf):
                            return eng.tensor_tensor(
                                obuf[qrow:qrow + 64,
                                     fc(qch, qlo + 512 * half):
                                     fc(qch, qlo + 512 * half) + 512],
                                PSC[0:64, 512 * half:512 * half + 512],
                                RBCS[0:64, :], ALU.mult)
                        emit("vector", faev, waits=[W("sc_rbc")],
                             reads=("psc", "rbcs") if half == 1 else ("rbcs",),
                             inc="v_aev")

        def gen_wo(phase):
            src = BB if phase == "g" else BC
            res = XN if phase == "g" else BA
            dst = BA if phase == "g" else BB
            incname = "v_h1" if phase == "g" else "v_s"
            wosem = "d_wgo" if phase == "g" else "d_wlo"
            for w4 in range(4):
                t0 = 512 * w4
                for mc in range(NCH):
                    half = mc % 2
                    slot = "wops%d" % half
                    def fwo(eng, mc=mc, t0=t0, half=half, src=src):
                        i = None
                        for c in range(NCH):
                            i = eng.matmul(
                                PSA[:, 512 * half:512 * half + 512],
                                BW[:, 1024 * c + 128 * mc:1024 * c + 128 * mc + 128],
                                src[:, fc(c, t0):fc(c, t0) + 512],
                                start=(c == 0), stop=(c == NCH - 1))
                        return i
                    emit("tensor", fwo, waits=[W(wosem, 64)],
                         writes=(slot,), inc="pe_wo")
                    def fh1(eng, mc=mc, t0=t0, half=half, res=res, dst=dst):
                        return eng.scalar_tensor_tensor(
                            dst[:, fc(mc, t0):fc(mc, t0) + 512],
                            PSA[:, 512 * half:512 * half + 512], 1.0,
                            res[:, fc(mc, t0):fc(mc, t0) + 512],
                            ALU.bypass, ALU.add)
                    emit("vector", fh1, waits=[W("pe_wo")], reads=(slot,),
                         inc=incname)

        if stage >= 2:
            phase_guard("prep_g")
            gen_prep("g")
        if stage >= 3:
            wload(0, dgwo, D, D, sem="d_wgo", waits=[W("pe_q")])
            wload(8192, dlwk, D, LAT, sem="d_wlk", waits=[W("pe_lat")])
            wload(11264, dlwv, D, LAT, sem="d_wlv", waits=[W("pe_v")])
            phase_guard("attn_g")
            gen_attn("g")
        if stage >= 4:
            phase_guard("wo_g")
            gen_wo("g")
            wload(0, dlwq, D, D, sem="d_wlq", waits=[W("pe_wo")])
        if stage >= 5:
            phase_guard("prep_l")
            gen_prep("l")
        if stage >= 6:
            wload(0, dlwo, D, D, sem="d_wlo", waits=[W("pe_q")])
            phase_guard("attn_l")
            gen_attn("l")
        if stage >= 7:
            phase_guard("wo_l")
            gen_wo("l")

        if stage >= 8:
            emit("sync", lambda eng: eng.dma_start(IND8[:], c_ind8.ap()),
                 waits=[W("v_s")], inc="d_ind8", ninc=16)
            phase_guard("router")
            for T in range(16):
                def flg(eng, T=T):
                    i = None
                    for c in range(NCH):
                        i = eng.matmul(PSD[:, 0:8],
                                       XN[:, fc(c, 128 * T):fc(c, 128 * T) + 128],
                                       RWT[:, 8 * c:8 * c + 8],
                                       start=(c == 0), stop=(c == NCH - 1))
                    return i
                emit("tensor", flg, waits=[W("d_rwt", 16)], writes=("psd",),
                     inc="pe_lg")
                emit("vector", lambda eng: eng.tensor_copy(LGT[:, 0:8],
                                                           PSD[:, 0:8]),
                     waits=[W("pe_lg")], reads=("psd",), inc="v_lgc")
                def ftm(eng):
                    eng.tensor_copy(LGT[:, 8:9], LGT[:, 0:1])
                    eng.memset(LGT[:, 16:17], -1e30)
                    i = None
                    for ci in range(1, 8):
                        pa, pb = LGT[:, 7 + ci:8 + ci], LGT[:, 15 + ci:16 + ci]
                        na, nb = LGT[:, 8 + ci:9 + ci], LGT[:, 16 + ci:17 + ci]
                        xcol = LGT[:, ci:ci + 1]
                        eng.tensor_tensor(LGT[:, 24:25], pa, xcol, ALU.min)
                        eng.tensor_tensor(na, pa, xcol, ALU.max)
                        i = eng.tensor_tensor(nb, pb, LGT[:, 24:25], ALU.max)
                    return i
                emit("vector", ftm, waits=[W("v_lgc")],
                     writes=("lgt",), inc="v_net")
                emit("vector", lambda eng: eng.tensor_scalar(
                    MKT[:], LGT[:, 0:8], LGT[:, 23:24], None, ALU.is_ge),
                    waits=[W("v_net")], inc="v_tm")
                emit("scalar", lambda eng: eng.activation(ZTM[:], LGT[:, 0:8],
                                                          AF.Exp, bias=0.0,
                                                          scale=1.0),
                     waits=[W("v_tm")], inc="sc_z")
                emit("vector", lambda eng: eng.tensor_tensor(
                    LGT[:, 32:40], ZTM[:], MKT[:], ALU.mult),
                    waits=[W("sc_z")], inc="v_zm")
                emit("vector", lambda eng: eng.reduce_sum(
                    LGT[:, 40:41], LGT[:, 32:40], axis=AX),
                    waits=[W("v_zm")], inc="v_dsum")
                emit("vector", lambda eng, T=T: eng.reciprocal(
                    RCPD[:, T:T + 1], LGT[:, 40:41]),
                    waits=[W("v_dsum")], reads=("lgt",), inc="v_route")
                emit("tensor", lambda eng: eng.transpose(PST[0:8, 0:128], MKT[:],
                                                         IDENT[:]),
                     waits=[W("v_route")], writes=("pst",), inc="pe_mtp")
                emit("vector", lambda eng, T=T: eng.tensor_copy(
                    MKF[0:8, 128 * T:128 * T + 128], PST[0:8, 0:128]),
                    waits=[W("pe_mtp")], reads=("pst",), inc="v_mkf")
            for sp in range(4):
                def flgf(eng, sp=sp):
                    i = None
                    for c in range(NCH):
                        i = eng.matmul(PSD[0:8, 0:512], RWT[:, 8 * c:8 * c + 8],
                                       XN[:, fc(c, 512 * sp):fc(c, 512 * sp) + 512],
                                       start=(c == 0), stop=(c == NCH - 1))
                    return i
                emit("tensor", flgf, waits=[W("v_mkf", 16)], writes=("psd",),
                     inc="pe_lgf")
                emit("scalar", lambda eng: eng.activation(
                    SRZ[:], PSD[0:8, 0:512], AF.Exp, bias=0.0, scale=CC[0:8, 5:6]),
                    waits=[W("pe_lgf")], reads=("psd",), writes=("srz",),
                    inc="sc_srz")
                def fszm(eng, sp=sp):
                    return eng.tensor_tensor(SZM[0:8, 512 * sp:512 * sp + 512],
                                             SRZ[:],
                                             MKF[0:8, 512 * sp:512 * sp + 512],
                                             ALU.mult)
                emit("vector", fszm, waits=[W("sc_srz")], reads=("srz",),
                     inc="v_szm")

        if stage >= 9:
            phase_guard("ffn")
            NU = 64
            for ps in range(8):
                t0 = 256 * ps
                pend_mm2, pend_hc = [], []
                for e2 in range(E):
                    def fwbc(eng, e2=e2, t0=t0):
                        return eng.matmul(
                            PSD[:, 0:256], IND8[:, 128 * e2:128 * e2 + 128],
                            SZM[0:8, t0:t0 + 256], start=True, stop=True)
                    emit("tensor", fwbc,
                         waits=[W("v_szm", 4), W("d_ind8", 16)],
                         writes=("psd",), inc="pe_wbc")
                    emit("scalar", lambda eng, e2=e2: eng.copy(
                        WBCS[:, 256 * e2:256 * e2 + 256], PSD[:, 0:256]),
                        waits=[W("pe_wbc")], reads=("psd",),
                        writes=("wbcs%d" % e2,), inc="sc_wbc")
                for u in range(NU):
                    e, oct_ = u // 8, u % 8
                    ub = 4096 * (u % 4)
                    bufslot = "bw%d" % (u % 4)
                    uw = [W("d_conv")]
                    if ps == 0 and u < 4:
                        uw += [W("pe_wo"), W("pe_q")]
                    def fw1(eng, e=e, oct_=oct_, ub=ub):
                        return eng.dma_start(
                            BW[:, ub:ub + 2048].rearrange("p (c n) -> p c n", n=256),
                            dw1b[e, :, 256 * oct_:256 * oct_ + 256]
                            .rearrange("(c p) n -> p c n", p=128))
                    def fw2(eng, e=e, oct_=oct_, ub=ub):
                        return eng.dma_start(
                            BW[:, ub + 2048:ub + 4096]
                            .rearrange("p (c n) -> p c n", n=1024),
                            dw2b[e, 256 * oct_:256 * oct_ + 256, :]
                            .rearrange("(c p) n -> p c n", p=128))
                    fsem = "d_fw%d" % (u % 4)
                    emit("sync", fw1, waits=uw, writes=(bufslot,),
                         inc=fsem, ninc=16)
                    emit("sync", fw2, waits=[], inc=fsem, ninc=16)
                    wld_u = counts[fsem]
                    p2cnt = {}
                    def emit_mm1(hc, u=u, ub=ub, bufslot=bufslot, e=e,
                                 fsem=fsem, wld_u=wld_u, t0=t0, p2cnt=p2cnt):
                        gh = (2 * u + hc) % 4
                        pcol = 512 * (gh % 2)
                        hslot = "ffnh%d" % (gh % 2)
                        def fmm1(eng, ub=ub, hc=hc, t0=t0, pcol=pcol):
                            i = None
                            for c in range(NCH):
                                i = eng.matmul(
                                    PSA[:, pcol:pcol + 256],
                                    BW[:, ub + 256 * c + 128 * hc:
                                       ub + 256 * c + 128 * hc + 128],
                                    XN[:, fc(c, t0):fc(c, t0) + 256],
                                    start=(c == 0), stop=(c == NCH - 1))
                            return i
                        emit("tensor", fmm1, waits=[(fsem, wld_u)],
                             reads=(bufslot,), writes=(hslot,), inc="pe_mm1")
                        mm1c = counts["pe_mm1"]
                        def fp1(eng, e=e, gh=gh, pcol=pcol):
                            return eng.scalar_tensor_tensor(
                                ASCR[:, 256 * (gh % 2):256 * (gh % 2) + 256],
                                PSA[:, pcol:pcol + 256], 0.0,
                                WBCS[:, 256 * e:256 * e + 256],
                                ALU.max, ALU.mult)
                        emit("vector", fp1,
                             waits=[W("pe_mm1", mm1c), W("sc_wbc")],
                             reads=(hslot, "wbcs%d" % e),
                             writes=("ascr%d" % (gh % 2),), inc="v_p1")
                        p1c = counts["v_p1"]
                        def fp2(eng, gh=gh):
                            a = ASCR[:, 256 * (gh % 2):256 * (gh % 2) + 256]
                            return eng.tensor_tensor(
                                HBF[:, 256 * gh:256 * gh + 256], a, a, ALU.mult)
                        emit("gpsimd", fp2, waits=[W("v_p1", p1c)],
                             reads=("ascr%d" % (gh % 2),),
                             writes=("hbf%d" % gh,), inc="po_p2")
                        p2cnt[hc] = counts["po_p2"]
                    def emit_mm2(hc, p2cnt=p2cnt, u=u, ub=ub, bufslot=bufslot):
                        gh = (2 * u + hc) % 4
                        def fmm2(eng, ub=ub, hc=hc, u=u, gh=gh):
                            i = None
                            for j in range(2):
                                yp = PSB if j == 0 else PSC
                                for uu in range(2):
                                    i = eng.matmul(
                                        yp[:, 512 * uu:512 * uu + 512],
                                        HBF[:, 256 * gh + 128 * j:
                                            256 * gh + 128 * j + 128],
                                        BW[:, ub + 2048 + 1024 * hc + 512 * uu:
                                           ub + 2048 + 1024 * hc + 512 * uu + 512],
                                        start=(u == 0 and hc == 0),
                                        stop=(u == NU - 1 and hc == 1))
                            return i
                        m2w = [W("po_p2", p2cnt[hc])]
                        if u == 0 and hc == 0:
                            m2w.append(("sc_g", counts["sc_g"]))
                        emit("tensor", fmm2, waits=m2w,
                             reads=("hbf%d" % ((2 * u + hc) % 4), bufslot),
                             writes=("psbc",) if (u == 0 and hc == 0) else (),
                             inc="pe_mm2")
                    for hc in range(2):
                        emit_mm1(hc)
                        pend_mm2.append(emit_mm2)
                        pend_hc.append(hc)
                        if len(pend_mm2) > 2:
                            fn_u = pend_mm2.pop(0)
                            fn_u(pend_hc.pop(0))
                while pend_mm2:
                    fn_u = pend_mm2.pop(0)
                    fn_u(pend_hc.pop(0))
                for j in range(2):
                    T = 2 * ps + j
                    yp = PSB if j == 0 else PSC
                    emit("sync", lambda eng, T=T: eng.dma_start(
                        XRL[:], dx[128 * T:128 * T + 128, :]),
                        waits=[W("v_ob")], inc="d_xr", ninc=16)
                    def fg(eng, T=T, yp=yp):
                        return eng.activation(GSB[:], yp[:], AF.Silu, bias=0.0,
                                              scale=RCPD[:, T:T + 1])
                    emit("scalar", fg,
                         waits=[W("pe_mm2"), W("v_route", 16), W("d_out")],
                         reads=("psbc",), writes=("gsb",), inc="sc_g")
                    def fstp(eng, T=T):
                        i = None
                        for c in range(NCH):
                            i = eng.transpose(PST[:, 128 * c:128 * c + 128],
                                              BB[:, fc(c, 128 * T):fc(c, 128 * T)
                                                 + 128], IDENT[:])
                        return i
                    emit("tensor", fstp, waits=[W("v_s")], writes=("pst",),
                         inc="pe_stp")
                    def foa(eng):
                        return eng.tensor_tensor(OUTA[:], PST[:], GSB[:], ALU.mult)
                    emit("vector", foa, waits=[W("pe_stp"), W("sc_g")],
                         reads=("pst", "gsb"), writes=("outa",), inc="v_oa")
                    def fob(eng):
                        return eng.tensor_tensor(OUTB[:], OUTA[:], XRL[:], ALU.add)
                    emit("vector", fob, waits=[W("d_xr"), W("d_out")],
                         reads=("outa", "xrl"), writes=("outb",), inc="v_ob")
                    emit("sync", lambda eng, T=T: eng.dma_start(
                        dout[128 * T:128 * T + 128, :], OUTB[:]),
                        waits=[W("v_ob")], reads=("outb",), inc="d_out", ninc=16)

        tapsrc = dict(xn=XN, a1=ATILE, qg=BA, kg0=K0, kg1=K1, vg=VA, gout=BB,
                      h1=BA, ql=BB, lout=BC, s=BB, szm=SZM, mkf=MKF, rcpd=RCPD,
                      qro=QRO, qn=QN, smallt=SMALLT)
        if tapt:
            snap = [(s, c) for s, c in counts.items() if c > 0]
            for name, ap in tapt.items():
                src = tapsrc[name]
                def ftap(eng, ap=ap, src=src):
                    return eng.dma_start(
                        ap, src[0:ap.shape[0], 0:ap.shape[1]])
                emit("sync", ftap, waits=snap, inc="d_out", ninc=16)

        final_out = counts["d_out"]

        semnames = sorted({inc for (_, _, _, inc, _) in prog if inc}
                          | {s for (_, _, ws, _, _) in prog for (s, _) in ws})
        sems = {n: en(nc.semaphore(n)) for n in semnames}
        block = en(nc.Block())

        def runner(ename):
            def body(eng):
                for (engine, fn, waits, inc, ninc) in prog:
                    if engine != ename:
                        continue
                    for (s, v) in waits:
                        eng.wait_ge(sems[s], v)
                    last = fn(eng)
                    if inc is not None:
                        assert last is not None, (engine, inc)
                        last.then_inc(sems[inc], ninc)
                if ename == "sync" and final_out > 0:
                    eng.wait_ge(sems["d_out"], final_out)
            return body

        block.sync(runner("sync"))
        block.tensor(runner("tensor"))
        block.vector(runner("vector"))
        block.scalar(runner("scalar"))
        block.gpsimd(runner("gpsimd"))

    return nc, tapt


def _perm_eo(w, head=64):
    ncols = w.shape[-1]
    idx = np.arange(ncols).reshape(-1, head)
    out = np.concatenate([idx[:, 0::2].reshape(-1), idx[:, 1::2].reshape(-1)])
    return np.ascontiguousarray(w[..., out])


def qhead_perm():
    perm = np.zeros(1024, dtype=np.int64)
    for c in range(8):
        a = (c % 4) + 8 * (c // 4)
        for d in range(64):
            perm[128 * c + d] = 64 * a + d
            perm[128 * c + 64 + d] = 64 * (a + 4) + d
    return perm


def _perm_wo(w):
    return np.ascontiguousarray(np.asarray(w)[qhead_perm(), :])


def prep_inputs(inputs):
    w = {
        "n1w": np.ascontiguousarray(np.asarray(inputs["L0_attn_norm1_w"])
                                    .reshape(D, 1)),
        "n2w": np.ascontiguousarray(np.asarray(inputs["L0_attn_norm2_w"])
                                    .reshape(D, 1)),
        "gwq": _perm_eo(np.asarray(inputs["L0_attn_g_wq"])),
        "gkvd": np.ascontiguousarray(inputs["L0_attn_g_kv_down"]),
        "gkup": _perm_eo(np.asarray(inputs["L0_attn_g_k_up"])),
        "gvup": np.ascontiguousarray(inputs["L0_attn_g_v_up"]),
        "gwo": _perm_wo(inputs["L0_attn_g_wo"]),
        "lwq": _perm_eo(np.asarray(inputs["L0_attn_l_wq"])),
        "lwk": _perm_eo(np.asarray(inputs["L0_attn_l_wk"])),
        "lwv": np.ascontiguousarray(inputs["L0_attn_l_wv"]),
        "lwo": _perm_wo(inputs["L0_attn_l_wo"]),
        "rwT": np.ascontiguousarray(np.asarray(inputs["L0_ffn_router_w"]).T),
        "w1": np.ascontiguousarray(inputs["L0_ffn_w1"]),
        "w2": np.ascontiguousarray(inputs["L0_ffn_w2"]),
    }
    w = {k: np.asarray(v, dtype=np.float32) for k, v in w.items()}
    xs = np.asarray(inputs["L0_attn_hidden"], dtype=np.float32)
    return [dict(w, x=np.ascontiguousarray(xs[b])) for b in range(8)]


def kernel(**inputs):
    in_maps = prep_inputs(inputs)
    nc, _ = build_nc()
    res = run_bass_kernel_spmd(nc, in_maps, core_ids=list(range(8)))
    return np.stack([r["out"] for r in res.results], axis=0)


# version 06
# speedup vs baseline: 1.3534x; 1.0328x over previous
import numpy as np
import ml_dtypes
import concourse.bass as bass
import concourse.mybir as mybir
from concourse.bass_utils import run_bass_kernel_spmd
from contextlib import ExitStack
from collections import defaultdict

F32 = mybir.dt.float32
BF16 = mybir.dt.bfloat16
FP16 = mybir.dt.float16
AF = mybir.ActivationFunctionType
ALU = mybir.AluOpType
AX = mybir.AxisListType.X

S, D, NCH = 2048, 1024, 8
H, KVH, HD, LAT, WIN = 16, 4, 64, 256, 256
E, HF = 8, 2048
NEG = -30000.0

OFF_XN, OFF_A, OFF_B, OFF_C, OFF_W, OFF_K, OFF_T = (
    0, 32768, 65536, 98304, 131072, 163840, 196608)
ARENA = 208896
PREP_SUB = 9
TEST_WLOADS = True
TEST_ATTN = True
ENGINES = ("sync", "tensor", "vector", "scalar", "gpsimd")


def _host_constants():
    c = {}
    c["ident"] = np.eye(128, dtype=ml_dtypes.bfloat16)
    kk = np.arange(128)[:, None]
    jj = np.arange(128)[None, :]
    c["mc"] = np.where(jj >= kk, 0.0, NEG).astype(np.float32)
    jj2 = np.arange(384)[None, :]
    c["ml"] = np.where((jj2 >= kk) & (jj2 < kk + WIN), 0.0, NEG).astype(np.float32)
    for name, base in (("g", 1_000_000.0), ("l", 10_000.0)):
        inv = 1.0 / base ** (np.arange(0, HD, 2, dtype=np.float64) / HD)
        fr = np.arange(S, dtype=np.float64)[:, None] * inv[None, :]
        cos = np.tile(np.cos(fr), (1, 16))
        sin = np.tile(np.sin(fr), (1, 16))
        c[f"rep_{name}"] = np.concatenate([cos, sin], axis=1).astype(np.float16)
    ind = np.zeros((8, 8 * 128), dtype=np.float32)
    for e in range(8):
        ind[e, 128 * e:128 * e + 128] = 1.0
    c["ind8"] = ind
    c["ones128_bf"] = np.ones((128, 1), dtype=ml_dtypes.bfloat16)
    c["ones1_f32"] = np.ones((1, 128), dtype=np.float32)
    return c


def _col_segs(j0, j1):
    segs = []
    for b0 in (0, 512):
        lo, hi = max(j0, b0), min(j1, b0 + 512)
        if lo < hi:
            segs.append((lo, hi))
    return segs


def build_nc(taps=(), stage=99):
    nc = bass.Bass("TRN2", target_bir_lowering=False, debug=False,
                   detect_race_conditions=False)
    cst = _host_constants()

    dx = nc.dram_tensor("x", [S, D], F32, kind="ExternalInput").ap()
    dn1 = nc.dram_tensor("n1w", [D, 1], F32, kind="ExternalInput").ap()
    dn2 = nc.dram_tensor("n2w", [D, 1], F32, kind="ExternalInput").ap()
    dgwq = nc.dram_tensor("gwq", [D, D], F32, kind="ExternalInput").ap()
    dgkvd = nc.dram_tensor("gkvd", [D, LAT], F32, kind="ExternalInput").ap()
    dgkup = nc.dram_tensor("gkup", [LAT, LAT], F32, kind="ExternalInput").ap()
    dgvup = nc.dram_tensor("gvup", [LAT, LAT], F32, kind="ExternalInput").ap()
    dgwo = nc.dram_tensor("gwo", [D, D], F32, kind="ExternalInput").ap()
    dlwq = nc.dram_tensor("lwq", [D, D], F32, kind="ExternalInput").ap()
    dlwk = nc.dram_tensor("lwk", [D, LAT], F32, kind="ExternalInput").ap()
    dlwv = nc.dram_tensor("lwv", [D, LAT], F32, kind="ExternalInput").ap()
    dlwo = nc.dram_tensor("lwo", [D, D], F32, kind="ExternalInput").ap()
    drwT = nc.dram_tensor("rwT", [D, E], F32, kind="ExternalInput").ap()
    dw1 = nc.dram_tensor("w1", [E, D, HF], F32, kind="ExternalInput").ap()
    dw2 = nc.dram_tensor("w2", [E, HF, D], F32, kind="ExternalInput").ap()
    dout = nc.dram_tensor("out", [S, D], F32, kind="ExternalOutput").ap()
    dw1b = nc.dram_tensor("w1b", [E, D, HF], BF16, kind="Internal").ap()
    dw2b = nc.dram_tensor("w2b", [E, HF, D], BF16, kind="Internal").ap()

    tapt = {}
    tapshape = dict(xn=([128, 16384], BF16), a1=([128, 2048], BF16),
                    qg=([128, 16384], BF16), kg0=([128, 2048], BF16),
                    kg1=([128, 2048], BF16), vg=([128, 4160], BF16),
                    gout=([128, 16384], BF16), h1=([128, 16384], BF16),
                    ql=([128, 16384], BF16), lout=([128, 16384], BF16),
                    s=([128, 16384], BF16), szm=([8, 2048], F32),
                    mkf=([8, 2048], BF16), rcpd=([128, 16], F32),
                    qps=([128, 1024], BF16), qro=([128, 1024], F32),
                    qn=([128, 1024], BF16), smallt=([128, 96], F32))
    for name in taps:
        sh, dt = tapshape[name]
        tapt[name] = nc.dram_tensor("tap_" + name, sh, dt,
                                    kind="ExternalOutput").ap()

    c_ident = nc.inline_tensor(np.asarray(cst["ident"]), "c_ident")
    c_mc = nc.inline_tensor(cst["mc"], "c_mc")
    c_ml = nc.inline_tensor(cst["ml"], "c_ml")
    c_repg = nc.inline_tensor(np.asarray(cst["rep_g"]), "c_repg")
    c_repl = nc.inline_tensor(np.asarray(cst["rep_l"]), "c_repl")
    c_ind8 = nc.inline_tensor(cst["ind8"], "c_ind8")
    c_o128 = nc.inline_tensor(np.asarray(cst["ones128_bf"]), "c_o128")
    c_o1 = nc.inline_tensor(cst["ones1_f32"], "c_o1")

    with ExitStack() as stk:
        en = stk.enter_context
        arena = nc.alloc_sbuf_tensor("arena", [128, ARENA], mybir.dt.uint8)
        abase = nc.lookup_mloc(arena).addr

        def at(name, off, shape, dtype):
            return nc.alloc_sbuf_tensor_at(name, shape, dtype, offset=abase + off)

        XN = at("XN", OFF_XN, [128, 16384], BF16)
        BA = at("BA", OFF_A, [128, 16384], BF16)
        BB = at("BB", OFF_B, [128, 16384], BF16)
        BC = at("BC", OFF_C, [128, 16384], BF16)
        BW = at("BW", OFF_W, [128, 16384], BF16)

        K0 = at("K0", OFF_K, [128, 2048], BF16)
        K1 = at("K1", OFF_K + 4096, [128, 2048], BF16)
        VA = at("VA", OFF_K + 8192, [128, 4160], BF16)
        ROPE = at("ROPE", OFF_K + 16512, [128, 2048], FP16)
        QSQ = at("QSQ", OFF_K + 20608, [128, 1024], BF16)
        QRO = at("QRO", OFF_K + 22656, [128, 1024], F32)
        QN = at("QN", OFF_K + 26752, [128, 1024], BF16)
        LATS = at("LATS", OFF_K + 28800, [128, 512], BF16)
        KSQ = at("KSQ", OFF_K + 29824, [128, 256], BF16)
        KRO = at("KRO", OFF_K + 30336, [128, 256], F32)
        KN = at("KN", OFF_K + 31360, [128, 256], BF16)
        PROBS = at("PROBS", OFF_K + 20608, [128, 2048], BF16)
        MSCR = at("MSCR", OFF_K + 24704, [128, 384], F32)
        RBCS = at("RBCS", OFF_K + 26240, [128, 512], F32)
        RDR = at("RDR", OFF_K + 28288, [1, 1024], F32)
        SRZ = at("SRZ", OFF_K, [8, 512], F32)
        SZM = at("SZM", OFF_K + 2048, [8, 2048], F32)
        MKF = at("MKF", OFF_K + 10240, [8, 2048], BF16)
        WBCS = at("WBCS", OFF_K + 14336, [128, 2048], F32)

        XST = at("XST", OFF_T, [128, 2048], BF16)
        ATILE = at("ATILE", OFF_T, [128, 2048], BF16)
        GSB = at("GSB", OFF_T, [128, 1024], F32)
        OUTB = at("OUTB", OFF_T, [128, 1024], F32)
        LGT = at("LGT", OFF_T, [128, 48], F32)
        MKT = at("MKT", OFF_T + 192, [128, 8], BF16)
        ZTM = at("ZTM", OFF_T + 224, [128, 8], F32)
        XSQ = at("XSQ", OFF_T + 4096, [128, 512], BF16)
        OUTA = at("OUTA", OFF_T + 4096, [128, 1024], F32)
        XNT = at("XNT", OFF_T + 5120, [128, 256], F32)
        ROWS = at("ROWS", OFF_T + 6144, [1, 768], F32)
        PRR = at("PRR", OFF_T + 6144, [1, 512], F32)
        SMALLT = at("SMALLT", OFF_T + 5120, [128, 96], F32)
        RSC1 = at("RSC1", OFF_T + 8192, [128, 512], F32)
        RSC2 = at("RSC2", OFF_T + 10240, [128, 512], F32)
        XRL = at("XRL", OFF_T + 8192, [128, 1024], F32)
        IND8 = at("IND8", OFF_K + 22528, [8, 1024], F32)
        HBF = at("HBF", OFF_K + 26624, [128, 1024], BF16)
        ASCR = at("ASCR", OFF_K + 28672, [128, 512], F32)

        N1W = nc.alloc_sbuf_tensor("N1W", [128, 8], F32)
        N2W = nc.alloc_sbuf_tensor("N2W", [128, 8], F32)
        RWT = nc.alloc_sbuf_tensor("RWT", [128, 64], BF16)
        RCPD = nc.alloc_sbuf_tensor("RCPD", [128, 16], F32)
        CC = nc.alloc_sbuf_tensor("CC", [128, 8], F32)
        CM05 = nc.alloc_sbuf_tensor("CM05", [128, 1], F32)
        IDENT = nc.alloc_sbuf_tensor("IDENT", [128, 128], BF16)
        MCT = nc.alloc_sbuf_tensor("MCT", [128, 128], F32)
        MLT = nc.alloc_sbuf_tensor("MLT", [128, 384], F32)
        O128 = nc.alloc_sbuf_tensor("O128", [128, 1], BF16)
        ZCOL = nc.alloc_sbuf_tensor("ZCOL", [128, 65], BF16)
        O1 = nc.alloc_sbuf_tensor("O1", [1, 128], F32)
        assert nc.sbuf_bytes_remaining >= 0, nc.sbuf_bytes_remaining

        PSA = en(nc.psum_tensor("PSA", [128, 1024], F32))
        PSB = en(nc.psum_tensor("PSB", [128, 1024], F32))
        PSC = en(nc.psum_tensor("PSC", [128, 1024], F32))
        PSD = en(nc.psum_tensor("PSD", [128, 512], F32))
        PST = en(nc.psum_tensor("PST", [128, 1024], BF16))

        prog = []
        counts = defaultdict(int)
        guards = {}

        def emit(engine, fn, waits=(), inc=None, ninc=1, reads=(), writes=()):
            wl = list(waits)
            for slot in writes:
                g = guards.get(slot)
                if g is not None:
                    wl.append(g)
            wl = [(s, v) for (s, v) in wl if v > 0]
            dd = {}
            for s, v in wl:
                dd[s] = max(dd.get(s, 0), v)
            prog.append((engine, fn, sorted(dd.items()), inc, ninc))
            if inc is not None:
                counts[inc] += ninc
                for slot in reads:
                    guards[slot] = (inc, counts[inc])

        def W(s, v=None):
            return (s, counts[s] if v is None else v)

        def fc(c, t0):
            return 2048 * c + t0

        def phase_guard(name):
            snap = [(s, c) for s, c in counts.items()
                    if c > 0 and not s.startswith("d_")]
            for eng in ("tensor", "vector", "scalar"):
                emit(eng, lambda e: e.nop(), waits=snap)

        for dst, src in ((IDENT, c_ident), (MCT, c_mc), (MLT, c_ml),
                         (O128, c_o128), (O1, c_o1)):
            emit("sync", lambda eng, d=dst, s=src: eng.dma_start(d[:], s.ap()),
                 inc="d_tab", ninc=16)
        emit("sync", lambda eng: eng.dma_start(
            N1W[:].rearrange("p (c one) -> p c one", one=1),
            dn1.rearrange("(c p) one -> p c one", p=128),
            allow_slow_non_contiguous=True), inc="d_tab", ninc=16)
        emit("sync", lambda eng: eng.dma_start(
            N2W[:].rearrange("p (c one) -> p c one", one=1),
            dn2.rearrange("(c p) one -> p c one", p=128),
            allow_slow_non_contiguous=True), inc="d_tab", ninc=16)
        emit("gpsimd", lambda eng: eng.dma_start(
            RWT[:].rearrange("p (c f) -> p c f", f=8),
            drwT.rearrange("(c p) f -> p c f", p=128)), inc="d_rwt", ninc=16)

        def gen_consts(eng):
            eng.memset(CM05[:], -0.5)
            eng.memset(ZCOL[:], 0.0)
            for col, v in ((0, 1e-5), (1, 1e-6), (2, 1.0 / 1024), (3, 1.0 / 64),
                           (4, 0.125), (5, 0.5), (6, 1.0), (7, -1.0)):
                eng.memset(CC[:, col:col + 1], v)
            last = None
            for T in range(16):
                for g in range(4):
                    o = 260 * T + 65 * g + 64
                    last = eng.memset(VA[:, o:o + 1], 1.0)
            return last
        emit("vector", gen_consts, waits=[W("d_tab")], inc="v_cst")

        def wload(dst_c0, w_ap, kdim, ndim, sem="d_w", waits=()):
            nch = kdim // 128
            per = max(1, 2048 // ndim)
            first = True
            for c0 in range(0, nch, per):
                c1 = min(nch, c0 + per)
                def f(eng, dst_c0=dst_c0, w_ap=w_ap, ndim=ndim, c0=c0, c1=c1):
                    return eng.dma_start(
                        BW[:, dst_c0 + c0 * ndim:dst_c0 + c1 * ndim]
                          .rearrange("p (c n) -> p c n", n=ndim),
                        w_ap[128 * c0:128 * c1, :]
                        .rearrange("(c p) n -> p c n", p=128))
                emit("gpsimd", f, waits=waits if first else (), inc=sem, ninc=16)
                first = False

        wload(0, dgwq, D, D)
        wload(8192, dgkvd, D, LAT)
        wload(10240, dgkup, LAT, LAT)
        wload(10752, dgvup, LAT, LAT)
        WLD = {"g": counts["d_w"]}

        for t in range(16):
            sl = (t % 2) * 1024
            wt = [W("pe_xtp", t - 1)] if t >= 2 else []
            emit("gpsimd", lambda eng, t=t, sl=sl: eng.dma_start(
                XST[:, sl:sl + 1024], dx[128 * t:128 * t + 128, :]),
                waits=wt, inc="d_x%d" % (t % 2), ninc=16)
        for e in range(E):
            for k4 in range(4):
                emit("gpsimd", lambda eng, e=e, k4=k4: eng.dma_start(
                    dw1b[e, 256 * k4:256 * k4 + 256, :],
                    dw1[e, 256 * k4:256 * k4 + 256, :]), inc="d_conv", ninc=16)
                emit("gpsimd", lambda eng, e=e, k4=k4: eng.dma_start(
                    dw2b[e, 512 * k4:512 * k4 + 512, :],
                    dw2[e, 512 * k4:512 * k4 + 512, :]), inc="d_conv", ninc=16)
        for t in range(16):
            sl = (t % 2) * 1024
            def ftp(eng, sl=sl):
                for c in range(NCH):
                    i = eng.transpose(PST[:, 128 * c:128 * c + 128],
                                      XST[:, sl + 128 * c:sl + 128 * c + 128],
                                      IDENT[:])
                return i
            emit("tensor", ftp,
                 waits=[W("d_tab", 112), W("d_x%d" % (t % 2), 16 * (t // 2 + 1))],
                 reads=("xst%d" % (t % 2),), writes=("pst",), inc="pe_xtp")
            def fev(eng, t=t):
                for c in range(NCH):
                    i = eng.tensor_copy(BA[:, fc(c, 128 * t):fc(c, 128 * t) + 128],
                                        PST[:, 128 * c:128 * c + 128])
                return i
            emit("vector", fev, waits=[W("pe_xtp")], reads=("pst",), inc="v_xfm")

        r0, r1, r2 = ROWS[0:1, 0:256], ROWS[0:1, 256:512], ROWS[0:1, 512:768]
        for sp in range(8):
            t0 = 256 * sp
            xfm_done = W("v_xfm", 2 * (sp + 1))
            def fsum(eng, t0=t0):
                for c in range(NCH):
                    i = eng.matmul(PSD[0:1, 0:256], O128[:],
                                   BA[:, fc(c, t0):fc(c, t0) + 256],
                                   start=(c == 0), stop=(c == NCH - 1))
                return i
            emit("tensor", fsum, waits=[xfm_done], writes=("psd",), inc="pe_lnsum")
            for c in range(NCH):
                sl = (c % 2) * 256
                def fsq(eng, c=c, sl=sl, t0=t0):
                    src = BA[:, fc(c, t0):fc(c, t0) + 256]
                    return eng.tensor_tensor(XSQ[:, sl:sl + 256], src, src, ALU.mult)
                emit("gpsimd", fsq, waits=[xfm_done],
                     reads=(), writes=("xsq%d" % (c % 2),), inc="po_sq")
                def fss(eng, c=c):
                    sl = (c % 2) * 256
                    return eng.matmul(PSC[0:1, 0:256], O128[:], XSQ[:, sl:sl + 256],
                                      start=(c == 0), stop=(c == NCH - 1))
                emit("tensor", fss, waits=[W("po_sq", 8 * sp + c + 1)],
                     reads=("xsq%d" % (c % 2),),
                     writes=("psc",) if c == 0 else (), inc="pe_lnss")
            emit("vector", lambda eng: eng.tensor_scalar(
                r0, PSD[0:1, 0:256], CC[0:1, 2:3], None, ALU.mult),
                waits=[W("pe_lnsum"), W("v_cst", 1)], reads=("psd",),
                inc="v_lnm")
            emit("vector", lambda eng: eng.tensor_tensor(r1, r0, r0, ALU.mult),
                 waits=[W("v_lnm")], inc="v_lnmsq")
            emit("vector", lambda eng: eng.scalar_tensor_tensor(
                r2, PSC[0:1, 0:256], CC[0:1, 2:3], r1, ALU.mult, ALU.subtract),
                waits=[W("pe_lnss"), W("v_lnmsq")], reads=("psc",),
                inc="v_lnvar")
            emit("scalar", lambda eng: eng.activation(
                r1, r2, AF.Ln, bias=CC[0:1, 0:1], scale=1.0),
                waits=[W("v_lnvar")], inc="sc_ln1")
            emit("scalar", lambda eng: eng.activation(
                r2, r1, AF.Exp, bias=0.0, scale=CM05[0:1, :]),
                waits=[W("sc_ln1")], inc="sc_lnsd")
            emit("vector", lambda eng: eng.tensor_tensor(r1, r0, r2, ALU.mult),
                 waits=[W("sc_lnsd")], inc="v_lnrow")
            def fbc(eng):
                eng.matmul(PSA[:, 0:256], O1[:], r2, start=True, stop=True)
                return eng.matmul(PSA[:, 512:768], O1[:], r1, start=True, stop=True)
            emit("tensor", fbc, waits=[W("v_lnrow")], writes=("psa",), inc="pe_lnbc")
            for c in range(NCH):
                def fxn(eng, c=c, t0=t0):
                    src = BA[:, fc(c, t0):fc(c, t0) + 256]
                    eng.tensor_tensor(XNT[:], src, PSA[:, 0:256], ALU.mult)
                    return eng.tensor_tensor(XN[:, fc(c, t0):fc(c, t0) + 256],
                                             XNT[:], PSA[:, 512:768], ALU.subtract)
                emit("vector", fxn, waits=[W("pe_lnbc")],
                     reads=("psa",) if c == NCH - 1 else (), inc="v_xn")

        def gen_prep(phase):
            srcbuf = XN if phase == "g" else BA
            wcol = N1W if phase == "g" else N2W
            qdst = BA if phase == "g" else BB
            ro = 0 if phase == "g" else 1024
            sdrow, rrow = PRR[0:1, 256:512], PRR[0:1, 0:256]
            rldc = {}
            for pt in range(8):
                t256 = 256 * pt
                srcrdy = (W("v_xn", 8 * (pt + 1)) if phase == "g"
                          else W("v_h1", 8 * (pt // 2 + 1)))
                for c in range(NCH):
                    sl = (c % 2) * 256
                    def fsq(eng, c=c, sl=sl, t256=t256, srcbuf=srcbuf):
                        src = srcbuf[:, fc(c, t256):fc(c, t256) + 256]
                        return eng.tensor_tensor(XSQ[:, sl:sl + 256], src, src,
                                                 ALU.mult)
                    emit("gpsimd", fsq, waits=[srcrdy],
                         writes=("xsq%d" % (c % 2),), inc="po_psq")
                    def fss(eng, c=c):
                        sl = (c % 2) * 256
                        return eng.matmul(PSD[0:1, 0:256], O128[:], XSQ[:, sl:sl + 256],
                                          start=(c == 0), stop=(c == NCH - 1))
                    emit("tensor", fss, waits=[W("po_psq")],
                         reads=("xsq%d" % (c % 2),),
                         writes=("psd",) if c == 0 else (), inc="pe_pss")
                emit("scalar", lambda eng, sdrow=sdrow: eng.activation(
                    sdrow, PSD[0:1, 0:256], AF.Ln,
                    bias=CC[0:1, 1:2], scale=CC[0:1, 2:3]),
                    waits=[W("pe_pss")], reads=("psd",), inc="sc_psd0")
                emit("scalar", lambda eng, sdrow=sdrow, rrow=rrow: eng.activation(
                    rrow, sdrow, AF.Exp, bias=0.0, scale=CM05[0:1, :]),
                    waits=[W("sc_psd0")], inc="sc_psd")
                emit("tensor", lambda eng, rrow=rrow: eng.matmul(
                    PSC[:, 0:256], O1[:], rrow, start=True, stop=True),
                    waits=[W("sc_psd")], writes=("psc",), inc="pe_pbc")
                for c in range(NCH):
                    def fa(eng, c=c, t256=t256, srcbuf=srcbuf, wcol=wcol):
                        return eng.scalar_tensor_tensor(
                            ATILE[:, 256 * c:256 * c + 256],
                            srcbuf[:, fc(c, t256):fc(c, t256) + 256],
                            wcol[:, c:c + 1], PSC[:, 0:256], ALU.mult, ALU.mult)
                    emit("vector", fa, waits=[W("pe_pbc")],
                         reads=("psc",) if c == NCH - 1 else (),
                         writes=("atile",), inc="v_atile")
                atile_rdy = W("v_atile")
                if PREP_SUB < 1:
                    continue
                if phase == "g":
                    def flat(eng):
                        i = None
                        for lc in range(2):
                            for c in range(NCH):
                                i = eng.matmul(
                                    PSB[:, 512 * lc:512 * lc + 256],
                                    BW[:, 8192 + 256 * c + 128 * lc:
                                       8192 + 256 * c + 128 * lc + 128],
                                    ATILE[:, 256 * c:256 * c + 256],
                                    start=(c == 0), stop=(c == NCH - 1))
                        return i
                    emit("tensor", flat,
                         waits=[atile_rdy, W("d_w", WLD["g"])],
                         reads=("atile",), writes=("psb",), inc="pe_lat")
                    if PREP_SUB >= 1.5:
                        def flatev(eng):
                            eng.tensor_copy(LATS[:, 0:256], PSB[:, 0:256])
                            return eng.tensor_copy(LATS[:, 256:512], PSB[:, 512:768])
                        emit("vector", flatev, waits=[W("pe_lat")],
                             reads=("psb",), writes=("lats",), inc="v_latev")
                for j in range(2):
                    T = 2 * pt + j
                    rsl = 1024 * (T % 2)
                    rsem = "d_rope%d" % (T % 2)
                    def frld(eng, T=T, rsl=rsl, phase=phase):
                        return eng.dma_start(
                            ROPE[:, rsl:rsl + 1024],
                            (c_repg if phase == "g" else c_repl)
                            .ap()[128 * T:128 * T + 128, :])
                    emit("sync", frld, writes=("rope%d" % (T % 2),),
                         inc=rsem, ninc=16)
                    rldc[T] = counts[rsem]
                qmc = {}
                for j in range(2):
                    qp, qslot = (PSA, "psa") if j == 0 else (PSB, "psb")
                    def fq(eng, j=j, qp=qp):
                        i = None
                        for c in range(NCH):
                            for u in range(2):
                                i = eng.matmul(
                                    qp[:, 512 * u:512 * u + 512],
                                    ATILE[:, 256 * c + 128 * j:256 * c + 128 * j + 128],
                                    BW[:, 1024 * c + 512 * u:1024 * c + 512 * u + 512],
                                    start=(c == 0), stop=(c == NCH - 1))
                        return i
                    qw = [atile_rdy,
                          (W("d_w", WLD["g"]) if phase == "g" else W("d_wlq", 64))]
                    emit("tensor", fq,
                         waits=qw, reads=("atile",), writes=(qslot,), inc="pe_q")
                    qmc[j] = counts["pe_q"]
                def fk(eng, j=0, phase=phase):
                    i = None
                    if phase == "g":
                        for lc in range(2):
                            i = eng.matmul(
                                PSD[:, 0:256],
                                LATS[:, 256 * lc + 128 * j:256 * lc + 128 * j + 128],
                                BW[:, 10240 + 256 * lc:10240 + 256 * lc + 256],
                                start=(lc == 0), stop=(lc == 1))
                    else:
                        for c in range(NCH):
                            i = eng.matmul(
                                PSD[:, 0:256],
                                ATILE[:, 256 * c + 128 * j:256 * c + 128 * j + 128],
                                BW[:, 8192 + 256 * c:8192 + 256 * c + 256],
                                start=(c == 0), stop=(c == NCH - 1))
                    return i
                def fv(eng, j=0, phase=phase):
                    i = None
                    if phase == "g":
                        for lc in range(2):
                            i = eng.matmul(
                                PSC[:, 512:768],
                                LATS[:, 256 * lc + 128 * j:256 * lc + 128 * j + 128],
                                BW[:, 10752 + 256 * lc:10752 + 256 * lc + 256],
                                start=(lc == 0), stop=(lc == 1))
                    else:
                        for c in range(NCH):
                            i = eng.matmul(
                                PSC[:, 512:768],
                                ATILE[:, 256 * c + 128 * j:256 * c + 128 * j + 128],
                                BW[:, 11264 + 256 * c:11264 + 256 * c + 256],
                                start=(c == 0), stop=(c == NCH - 1))
                    return i
                import functools
                kvsrc = ("lats",) if phase == "g" else ("atile",)
                kw_extra = [] if phase == "g" else [W("d_wlk", 16)]
                vw_extra = [] if phase == "g" else [W("d_wlv", 16)]
                kvdep = [W("v_latev")] if phase == "g" else [atile_rdy]
                emit("tensor", functools.partial(fk, j=0),
                     waits=kvdep + kw_extra, reads=kvsrc, writes=("psd",),
                     inc="pe_k")
                kmc = {0: counts["pe_k"]}
                emit("tensor", functools.partial(fv, j=0),
                     waits=kvdep + vw_extra, reads=kvsrc, writes=("psc5",),
                     inc="pe_v")
                vmc = {0: counts["pe_v"]}
                for j in range(2):
                    T = 2 * pt + j
                    rsl = 1024 * (T % 2)
                    rsem = "d_rope%d" % (T % 2)
                    rld = rldc[T]
                    qp, qslot = (PSA, "psa") if j == 0 else (PSB, "psb")
                    def fqro(eng, qp=qp, rsl=rsl):
                        cosS = ROPE[:, rsl:rsl + 512]
                        sinS = ROPE[:, rsl + 512:rsl + 1024]
                        e, o = qp[:, 0:512], qp[:, 512:1024]
                        eng.tensor_tensor(RSC1[:], e, cosS, ALU.mult)
                        eng.tensor_tensor(RSC2[:], o, sinS, ALU.mult)
                        eng.tensor_tensor(QRO[:, 0:512], RSC1[:], RSC2[:],
                                          ALU.subtract)
                        eng.tensor_tensor(RSC1[:], e, sinS, ALU.mult)
                        eng.tensor_tensor(RSC2[:], o, cosS, ALU.mult)
                        return eng.tensor_tensor(QRO[:, 512:1024], RSC1[:], RSC2[:],
                                                 ALU.add)
                    emit("vector", fqro,
                         waits=[W("pe_q", qmc[j]), (rsem, rld)],
                         reads=(qslot, "rope%d" % (T % 2)), writes=("qro",),
                         inc="v_qro2")
                    emit("gpsimd", lambda eng: eng.tensor_tensor(
                        QSQ[:], QRO[:], QRO[:], ALU.mult),
                        waits=[W("v_qro2")], reads=("qro",), writes=("qsq",),
                        inc="po_qsq")
                    def fqss(eng):
                        i = None
                        qv = QSQ.ap().rearrange("p (half rest) -> p half rest",
                                                half=2)
                        for h in range(H):
                            i = eng.reduce_sum(SMALLT[:, 64 + h:65 + h],
                                               qv[:, :, 32 * h:32 * h + 32],
                                               axis=mybir.AxisListType.XY)
                        return i
                    emit("vector", fqss, waits=[W("po_qsq")], reads=("qsq",),
                         writes=("smallt",), inc="v_qss")
                    emit("scalar", lambda eng: eng.activation(
                        SMALLT[:, 80:96], SMALLT[:, 64:80], AF.Ln,
                        bias=CC[:, 1:2], scale=CC[:, 3:4]),
                        waits=[W("v_qss")], reads=("smallt",), inc="sc_qs0")
                    emit("scalar", lambda eng: eng.activation(
                        SMALLT[:, 64:80], SMALLT[:, 80:96], AF.Exp,
                        bias=0.0, scale=CM05[:]),
                        waits=[W("sc_qs0")], inc="sc_qsd")
                    def fkro(eng, rsl=rsl):
                        cosS = ROPE[:, rsl:rsl + 128]
                        sinS = ROPE[:, rsl + 512:rsl + 640]
                        e, o = PSD[:, 0:128], PSD[:, 128:256]
                        eng.tensor_tensor(RSC1[:, 0:128], e, cosS, ALU.mult)
                        eng.tensor_tensor(RSC2[:, 0:128], o, sinS, ALU.mult)
                        eng.tensor_tensor(KRO[:, 0:128], RSC1[:, 0:128],
                                          RSC2[:, 0:128], ALU.subtract)
                        eng.tensor_tensor(RSC1[:, 0:128], e, sinS, ALU.mult)
                        eng.tensor_tensor(RSC2[:, 0:128], o, cosS, ALU.mult)
                        return eng.tensor_tensor(KRO[:, 128:256], RSC1[:, 0:128],
                                                 RSC2[:, 0:128], ALU.add)
                    emit("vector", fkro, waits=[W("pe_k", kmc[j]), (rsem, rld)],
                         reads=("psd", "rope%d" % (T % 2)), writes=("kro",),
                         inc="v_kro2")
                    if j == 0:
                        emit("tensor", functools.partial(fk, j=1),
                             waits=kvdep + kw_extra + [W("v_kro2")],
                             reads=kvsrc, writes=("psd",), inc="pe_k")
                        kmc[1] = counts["pe_k"]
                    emit("gpsimd", lambda eng: eng.tensor_tensor(
                        KSQ[:], KRO[:], KRO[:], ALU.mult),
                        waits=[W("v_kro2")], reads=("kro",), writes=("ksq",),
                        inc="po_ksq")
                    def fkss(eng):
                        i = None
                        kv2 = KSQ.ap().rearrange("p (half rest) -> p half rest",
                                                 half=2)
                        for g in range(KVH):
                            i = eng.reduce_sum(SMALLT[:, 48 + g:49 + g],
                                               kv2[:, :, 32 * g:32 * g + 32],
                                               axis=mybir.AxisListType.XY)
                        return i
                    emit("vector", fkss, waits=[W("po_ksq")], reads=("ksq",),
                         writes=("smalltk",), inc="v_kss")
                    emit("scalar", lambda eng: eng.activation(
                        SMALLT[:, 52:56], SMALLT[:, 48:52], AF.Ln,
                        bias=CC[:, 1:2], scale=CC[:, 3:4]),
                        waits=[W("v_kss")], reads=("smalltk",), inc="sc_ks0")
                    emit("scalar", lambda eng: eng.activation(
                        SMALLT[:, 56:60], SMALLT[:, 52:56], AF.Exp,
                        bias=0.0, scale=CM05[:]),
                        waits=[W("sc_ks0")], inc="sc_ksd")
                    def fqne(eng):
                        i = None
                        for h in range(H):
                            cn = 128 * ((h % 4) + 4 * (h // 8)) + 64 * ((h // 4) % 2)
                            i = eng.tensor_scalar(QN[:, cn:cn + 32],
                                                  QRO[:, 32 * h:32 * h + 32],
                                                  SMALLT[:, 64 + h:65 + h], None,
                                                  ALU.mult)
                        return i
                    emit("vector", fqne, waits=[W("sc_qsd")], reads=("qro",),
                         writes=("qn",), inc="v_qn")
                    def fqno(eng):
                        i = None
                        for h in range(H):
                            cn = 128 * ((h % 4) + 4 * (h // 8)) + 64 * ((h // 4) % 2)
                            i = eng.tensor_scalar(QN[:, cn + 32:cn + 64],
                                                  QRO[:, 512 + 32 * h:512 + 32 * h + 32],
                                                  SMALLT[:, 64 + h:65 + h], None,
                                                  ALU.mult)
                        return i
                    emit("gpsimd", fqno, waits=[W("sc_qsd")], reads=("qro",),
                         writes=("qn",), inc="po_qn")
                    def fkn(eng):
                        i = None
                        for g in range(KVH):
                            i = eng.tensor_scalar(KN[:, 64 * g:64 * g + 32],
                                                  KRO[:, 32 * g:32 * g + 32],
                                                  SMALLT[:, 56 + g:57 + g], None,
                                                  ALU.mult)
                            i = eng.tensor_scalar(KN[:, 64 * g + 32:64 * g + 64],
                                                  KRO[:, 128 + 32 * g:128 + 32 * g + 32],
                                                  SMALLT[:, 56 + g:57 + g], None,
                                                  ALU.mult)
                        return i
                    emit("vector", fkn, waits=[W("sc_ksd")], reads=("kro",),
                         writes=("kn",), inc="v_kn")
                    def fqtp(eng):
                        i = None
                        for c in range(NCH):
                            i = eng.transpose(PST[:, 128 * c:128 * c + 128],
                                              QN[:, 128 * c:128 * c + 128], IDENT[:])
                        return i
                    emit("tensor", fqtp, waits=[W("v_qn"), W("po_qn")],
                         reads=("qn",), writes=("pst",), inc="pe_ptp")
                    def fqev(eng, T=T, qdst=qdst):
                        i = None
                        for c in range(NCH):
                            i = eng.tensor_copy(
                                qdst[:, fc(c, 128 * T):fc(c, 128 * T) + 128],
                                PST[:, 128 * c:128 * c + 128])
                        return i
                    emit("vector", fqev, waits=[W("pe_ptp")], reads=("pst",),
                         inc="v_ptev")
                    def fktp(eng):
                        eng.transpose(PST[:, 0:128], KN[:, 0:128], IDENT[:])
                        return eng.transpose(PST[:, 128:256], KN[:, 128:256], IDENT[:])
                    emit("tensor", fktp, waits=[W("v_kn")],
                         reads=("kn",), writes=("pst",), inc="pe_ptp")
                    def fkev(eng, T=T):
                        eng.tensor_copy(K0[:, 128 * T:128 * T + 128], PST[:, 0:128])
                        return eng.tensor_copy(K1[:, 128 * T:128 * T + 128],
                                               PST[:, 128:256])
                    emit("vector", fkev, waits=[W("pe_ptp")], reads=("pst",),
                         inc="v_ptev")
                    def fvev(eng, T=T):
                        i = None
                        for g in range(KVH):
                            i = eng.tensor_copy(
                                VA[:, 260 * T + 65 * g:260 * T + 65 * g + 64],
                                PSC[:, 512 + 64 * g:512 + 64 * g + 64])
                        return i
                    emit("vector", fvev, waits=[W("pe_v", vmc[j]), W("v_cst", 1)],
                         reads=("psc5",), inc="v_vev")
                    if j == 0:
                        emit("tensor", functools.partial(fv, j=1),
                             waits=kvdep + vw_extra + [W("v_vev")],
                             reads=kvsrc, writes=("psc5",), inc="pe_v")
                        vmc[1] = counts["pe_v"]

        def gen_attn(phase):
            qbuf = BA if phase == "g" else BB
            obuf = BB if phase == "g" else BC
            for h in range(H):
                kv = h // 4
                ktens = K0 if kv < 2 else K1
                krow = 64 * (kv % 2)
                qch = (h % 4) + 4 * (h // 8)
                qrow = 64 * ((h // 4) % 2)
                for qb in range(2):
                    qlo = 1024 * qb
                    steps = []
                    for kb in range(S // 128):
                        k0 = 128 * kb
                        if phase == "g":
                            lo = max(qlo, k0)
                            if lo >= qlo + 1024:
                                continue
                            j0, j1 = lo - qlo, 1024
                            if k0 >= qlo:
                                jm0, jm1 = j0, min(j0 + 128, j1)
                            else:
                                jm0 = jm1 = j0
                        else:
                            lo, hi = max(qlo, k0), min(qlo + 1024, k0 + 128 + WIN)
                            if lo >= hi:
                                continue
                            j0, j1 = lo - qlo, hi - qlo
                            jm0, jm1 = j0, j1
                        steps.append((kb, j0, j1, jm0, jm1))
                    nst = len(steps)
                    if phase == "l":
                        def fzero(eng):
                            eng.matmul(PSC[0:65, 0:512], ZCOL[:], K0[:, 0:512],
                                       start=True, stop=False)
                            return eng.matmul(PSC[0:65, 512:1024], ZCOL[:],
                                              K0[:, 0:512], start=True, stop=False)
                        emit("tensor", fzero, waits=[W("v_cst", 1)],
                             writes=("psc", "psc5"), inc="pe_pvz")
                    def mk_fsc(kb, j0, j1, scp):
                        def fsc(eng, kb=kb, j0=j0, j1=j1, scp=scp, ktens=ktens,
                                krow=krow, qch=qch, qrow=qrow, qlo=qlo, qbuf=qbuf):
                            i = None
                            for (a, b) in _col_segs(j0, j1):
                                i = eng.matmul(
                                    scp[:, a:b],
                                    ktens[krow:krow + 64, 128 * kb:128 * kb + 128],
                                    qbuf[qrow:qrow + 64,
                                         fc(qch, qlo + a):fc(qch, qlo + a) + (b - a)],
                                    start=True, stop=True)
                            return i
                        return fsc
                    sccnt = {}
                    def emit_sc(si):
                        kb, j0, j1, _, _ = steps[si]
                        scslot = "psa" if kb % 2 == 0 else "psb"
                        scp = PSA if kb % 2 == 0 else PSB
                        emit("tensor", mk_fsc(kb, j0, j1, scp), writes=(scslot,),
                             inc="pe_sc")
                        sccnt[si] = counts["pe_sc"]
                    emit_sc(0)
                    for si, (kb, j0, j1, jm0, jm1) in enumerate(steps):
                        scp, scslot = (PSA, "psa") if kb % 2 == 0 else (PSB, "psb")
                        pp0 = (kb % 2) * 1024
                        ppslot = "probs%d" % (kb % 2)
                        if si + 1 < len(steps):
                            emit_sc(si + 1)
                        if jm1 > jm0 and jm1 < j1:
                            mt = MCT if phase == "g" else MLT
                            mo = qlo + jm0 - 128 * kb
                            def fmask(eng, jm0=jm0, jm1=jm1, scp=scp, mt=mt, mo=mo):
                                eng.tensor_tensor(
                                    MSCR[:, 0:jm1 - jm0], scp[:, jm0:jm1],
                                    mt[:, mo:mo + jm1 - jm0], ALU.add)
                                return eng.tensor_copy(scp[:, jm0:jm1],
                                                       MSCR[:, 0:jm1 - jm0])
                            emit("vector", fmask,
                                 waits=[W("pe_sc", sccnt[si]), W("d_tab", 112)],
                                 writes=("mscr",), inc="v_msk")
                            def fexp(eng, j0=j0, j1=j1, scp=scp, pp0=pp0):
                                return eng.activation(
                                    PROBS[:, pp0 + j0:pp0 + j1], scp[:, j0:j1],
                                    AF.Exp, bias=0.0, scale=CC[:, 4:5])
                            emit("scalar", fexp,
                                 waits=[W("pe_sc", sccnt[si]), W("v_msk")],
                                 reads=(scslot, "mscr"), writes=(ppslot,),
                                 inc="sc_exp")
                        elif jm1 > jm0:
                            mt = MCT if phase == "g" else MLT
                            mo = qlo + jm0 - 128 * kb
                            def fmask(eng, jm0=jm0, jm1=jm1, scp=scp, mt=mt, mo=mo):
                                return eng.tensor_tensor(
                                    MSCR[:, 0:jm1 - jm0], scp[:, jm0:jm1],
                                    mt[:, mo:mo + jm1 - jm0], ALU.add)
                            emit("vector", fmask,
                                 waits=[W("pe_sc", sccnt[si]), W("d_tab", 112)],
                                 writes=("mscr",), inc="v_msk")
                            def fexp(eng, jm0=jm0, jm1=jm1, pp0=pp0):
                                return eng.activation(
                                    PROBS[:, pp0 + jm0:pp0 + jm1],
                                    MSCR[:, 0:jm1 - jm0], AF.Exp,
                                    bias=0.0, scale=CC[:, 4:5])
                            emit("scalar", fexp, waits=[W("v_msk")],
                                 reads=(scslot, "mscr"), writes=(ppslot,),
                                 inc="sc_exp")
                        else:
                            def fexp(eng, j0=j0, j1=j1, scp=scp, pp0=pp0):
                                return eng.activation(
                                    PROBS[:, pp0 + j0:pp0 + j1], scp[:, j0:j1],
                                    AF.Exp, bias=0.0, scale=CC[:, 4:5])
                            emit("scalar", fexp, waits=[W("pe_sc", sccnt[si])],
                                 reads=(scslot,), writes=(ppslot,), inc="sc_exp")
                        def fpv(eng, kb=kb, si=si, steps=tuple(steps), pp0=pp0,
                                kv=kv):
                            i = None
                            for (a, b) in _col_segs(steps[si][1], steps[si][2]):
                                bank = 0 if a < 512 else 1
                                first = True
                                for s2 in steps[:si]:
                                    if max(s2[1], 512 * bank) < min(s2[2],
                                                                    512 * bank + 512):
                                        first = False
                                        break
                                last = True
                                for s2 in steps[si + 1:]:
                                    if max(s2[1], 512 * bank) < min(s2[2],
                                                                    512 * bank + 512):
                                        last = False
                                        break
                                i = eng.matmul(
                                    PSC[0:65, a:b],
                                    VA[:, 260 * kb + 65 * kv:260 * kb + 65 * kv + 65],
                                    PROBS[:, pp0 + a:pp0 + b],
                                    start=(first and phase == "g"), stop=last)
                            return i
                        emit("tensor", fpv, waits=[W("sc_exp")],
                             reads=(ppslot,),
                             writes=("psc", "psc5") if si == 0 else (),
                             inc="pe_pv")
                    emit("scalar", lambda eng: eng.activation(
                        RDR[:], PSC[64:65, :], AF.Ln, bias=0.0, scale=1.0),
                        waits=[W("pe_pv")], writes=("rdr",), inc="v_rd")
                    for half in range(2):
                        def fbcd(eng, half=half):
                            return eng.matmul(PSD[:, 0:512], O1[:],
                                              RDR[0:1, 512 * half:512 * half + 512],
                                              start=True, stop=True)
                        emit("tensor", fbcd, waits=[W("v_rd")],
                             reads=("rdr",) if half == 1 else (),
                             writes=("psd",), inc="pe_rbc")
                        emit("scalar", lambda eng: eng.activation(
                            RBCS[:], PSD[:, 0:512], AF.Exp, bias=0.0,
                            scale=CC[:, 7:8]),
                            waits=[W("pe_rbc")], reads=("psd",), writes=("rbcs",),
                            inc="sc_rbc")
                        def faev(eng, half=half, qrow=qrow, qch=qch, qlo=qlo,
                                 obuf=obuf):
                            return eng.tensor_tensor(
                                obuf[qrow:qrow + 64,
                                     fc(qch, qlo + 512 * half):
                                     fc(qch, qlo + 512 * half) + 512],
                                PSC[0:64, 512 * half:512 * half + 512],
                                RBCS[0:64, :], ALU.mult)
                        emit("vector", faev, waits=[W("sc_rbc")],
                             reads=("psc", "psc5", "rbcs") if half == 1
                             else ("rbcs",), inc="v_aev")

        def gen_wo(phase):
            src = BB if phase == "g" else BC
            res = XN if phase == "g" else BA
            dst = BA if phase == "g" else BB
            incname = "v_h1" if phase == "g" else "v_s"
            wosem = "d_wgo" if phase == "g" else "d_wlo"
            for w4 in range(4):
                t0 = 512 * w4
                for mc in range(NCH):
                    half = mc % 2
                    slot = "wops%d" % half
                    def fwo(eng, mc=mc, t0=t0, half=half, src=src):
                        i = None
                        for c in range(NCH):
                            i = eng.matmul(
                                PSA[:, 512 * half:512 * half + 512],
                                BW[:, 1024 * c + 128 * mc:1024 * c + 128 * mc + 128],
                                src[:, fc(c, t0):fc(c, t0) + 512],
                                start=(c == 0), stop=(c == NCH - 1))
                        return i
                    emit("tensor", fwo, waits=[W(wosem, 64)],
                         writes=(slot,), inc="pe_wo")
                    def fh1(eng, mc=mc, t0=t0, half=half, res=res, dst=dst):
                        return eng.scalar_tensor_tensor(
                            dst[:, fc(mc, t0):fc(mc, t0) + 512],
                            PSA[:, 512 * half:512 * half + 512], 1.0,
                            res[:, fc(mc, t0):fc(mc, t0) + 512],
                            ALU.bypass, ALU.add)
                    emit("vector", fh1, waits=[W("pe_wo")], reads=(slot,),
                         inc=incname)

        if stage >= 2:
            phase_guard("prep_g")
            gen_prep("g")
        if stage >= 3:
            wload(0, dgwo, D, D, sem="d_wgo", waits=[W("pe_q")])
            wload(8192, dlwk, D, LAT, sem="d_wlk", waits=[W("pe_lat")])
            wload(11264, dlwv, D, LAT, sem="d_wlv", waits=[W("pe_v")])
            phase_guard("attn_g")
            gen_attn("g")
        if stage >= 4:
            phase_guard("wo_g")
            gen_wo("g")
            wload(0, dlwq, D, D, sem="d_wlq", waits=[W("pe_wo")])
        if stage >= 5:
            phase_guard("prep_l")
            gen_prep("l")
        if stage >= 6:
            wload(0, dlwo, D, D, sem="d_wlo", waits=[W("pe_q")])
            phase_guard("attn_l")
            gen_attn("l")
        if stage >= 7:
            phase_guard("wo_l")
            gen_wo("l")

        if stage >= 8:
            emit("sync", lambda eng: eng.dma_start(IND8[:], c_ind8.ap()),
                 waits=[W("v_s")], inc="d_ind8", ninc=16)
            phase_guard("router")
            for T in range(16):
                def flg(eng, T=T):
                    i = None
                    for c in range(NCH):
                        i = eng.matmul(PSD[:, 0:8],
                                       XN[:, fc(c, 128 * T):fc(c, 128 * T) + 128],
                                       RWT[:, 8 * c:8 * c + 8],
                                       start=(c == 0), stop=(c == NCH - 1))
                    return i
                emit("tensor", flg, waits=[W("d_rwt", 16)], writes=("psd",),
                     inc="pe_lg")
                emit("vector", lambda eng: eng.tensor_copy(LGT[:, 0:8],
                                                           PSD[:, 0:8]),
                     waits=[W("pe_lg")], reads=("psd",), inc="v_lgc")
                def ftm(eng):
                    eng.tensor_copy(LGT[:, 8:9], LGT[:, 0:1])
                    eng.memset(LGT[:, 16:17], -1e30)
                    i = None
                    for ci in range(1, 8):
                        pa, pb = LGT[:, 7 + ci:8 + ci], LGT[:, 15 + ci:16 + ci]
                        na, nb = LGT[:, 8 + ci:9 + ci], LGT[:, 16 + ci:17 + ci]
                        xcol = LGT[:, ci:ci + 1]
                        eng.tensor_tensor(LGT[:, 24:25], pa, xcol, ALU.min)
                        eng.tensor_tensor(na, pa, xcol, ALU.max)
                        i = eng.tensor_tensor(nb, pb, LGT[:, 24:25], ALU.max)
                    return i
                emit("vector", ftm, waits=[W("v_lgc")],
                     writes=("lgt",), inc="v_net")
                emit("vector", lambda eng: eng.tensor_scalar(
                    MKT[:], LGT[:, 0:8], LGT[:, 23:24], None, ALU.is_ge),
                    waits=[W("v_net")], inc="v_tm")
                emit("scalar", lambda eng: eng.activation(ZTM[:], LGT[:, 0:8],
                                                          AF.Exp, bias=0.0,
                                                          scale=1.0),
                     waits=[W("v_tm")], inc="sc_z")
                emit("vector", lambda eng: eng.tensor_tensor(
                    LGT[:, 32:40], ZTM[:], MKT[:], ALU.mult),
                    waits=[W("sc_z")], inc="v_zm")
                emit("vector", lambda eng: eng.reduce_sum(
                    LGT[:, 40:41], LGT[:, 32:40], axis=AX),
                    waits=[W("v_zm")], inc="v_dsum")
                emit("vector", lambda eng, T=T: eng.reciprocal(
                    RCPD[:, T:T + 1], LGT[:, 40:41]),
                    waits=[W("v_dsum")], reads=("lgt",), inc="v_route")
                emit("tensor", lambda eng: eng.transpose(PST[0:8, 0:128], MKT[:],
                                                         IDENT[:]),
                     waits=[W("v_route")], writes=("pst",), inc="pe_mtp")
                emit("vector", lambda eng, T=T: eng.tensor_copy(
                    MKF[0:8, 128 * T:128 * T + 128], PST[0:8, 0:128]),
                    waits=[W("pe_mtp")], reads=("pst",), inc="v_mkf")
            for sp in range(4):
                def flgf(eng, sp=sp):
                    i = None
                    for c in range(NCH):
                        i = eng.matmul(PSD[0:8, 0:512], RWT[:, 8 * c:8 * c + 8],
                                       XN[:, fc(c, 512 * sp):fc(c, 512 * sp) + 512],
                                       start=(c == 0), stop=(c == NCH - 1))
                    return i
                emit("tensor", flgf, waits=[W("v_mkf", 16)], writes=("psd",),
                     inc="pe_lgf")
                emit("scalar", lambda eng: eng.activation(
                    SRZ[:], PSD[0:8, 0:512], AF.Exp, bias=0.0, scale=CC[0:8, 5:6]),
                    waits=[W("pe_lgf")], reads=("psd",), writes=("srz",),
                    inc="sc_srz")
                def fszm(eng, sp=sp):
                    return eng.tensor_tensor(SZM[0:8, 512 * sp:512 * sp + 512],
                                             SRZ[:],
                                             MKF[0:8, 512 * sp:512 * sp + 512],
                                             ALU.mult)
                emit("vector", fszm, waits=[W("sc_srz")], reads=("srz",),
                     inc="v_szm")

        if stage >= 9:
            phase_guard("ffn")
            NU = 64
            for ps in range(8):
                t0 = 256 * ps
                pend_mm2, pend_hc = [], []
                for e2 in range(E):
                    def fwbc(eng, e2=e2, t0=t0):
                        return eng.matmul(
                            PSD[:, 0:256], IND8[:, 128 * e2:128 * e2 + 128],
                            SZM[0:8, t0:t0 + 256], start=True, stop=True)
                    emit("tensor", fwbc,
                         waits=[W("v_szm", 4), W("d_ind8", 16)],
                         writes=("psd",), inc="pe_wbc")
                    emit("scalar", lambda eng, e2=e2: eng.copy(
                        WBCS[:, 256 * e2:256 * e2 + 256], PSD[:, 0:256]),
                        waits=[W("pe_wbc")], reads=("psd",),
                        writes=("wbcs%d" % e2,), inc="sc_wbc")
                for u in range(NU):
                    e, oct_ = u // 8, u % 8
                    ub = 4096 * (u % 4)
                    bufslot = "bw%d" % (u % 4)
                    uw = [W("d_conv")]
                    if ps == 0 and u < 4:
                        uw += [W("pe_wo"), W("pe_q")]
                    def fw1(eng, e=e, oct_=oct_, ub=ub):
                        return eng.dma_start(
                            BW[:, ub:ub + 2048].rearrange("p (c n) -> p c n", n=256),
                            dw1b[e, :, 256 * oct_:256 * oct_ + 256]
                            .rearrange("(c p) n -> p c n", p=128))
                    def fw2(eng, e=e, oct_=oct_, ub=ub):
                        return eng.dma_start(
                            BW[:, ub + 2048:ub + 4096]
                            .rearrange("p (c n) -> p c n", n=1024),
                            dw2b[e, 256 * oct_:256 * oct_ + 256, :]
                            .rearrange("(c p) n -> p c n", p=128))
                    fsem = "d_fw%d" % (u % 4)
                    emit("sync", fw1, waits=uw, writes=(bufslot,),
                         inc=fsem, ninc=16)
                    emit("sync", fw2, waits=[], inc=fsem, ninc=16)
                    wld_u = counts[fsem]
                    p2cnt = {}
                    def emit_mm1(hc, u=u, ub=ub, bufslot=bufslot, e=e,
                                 fsem=fsem, wld_u=wld_u, t0=t0, p2cnt=p2cnt):
                        gh = (2 * u + hc) % 4
                        pcol = 512 * (gh % 2)
                        hslot = "ffnh%d" % (gh % 2)
                        def fmm1(eng, ub=ub, hc=hc, t0=t0, pcol=pcol):
                            i = None
                            for c in range(NCH):
                                i = eng.matmul(
                                    PSA[:, pcol:pcol + 256],
                                    BW[:, ub + 256 * c + 128 * hc:
                                       ub + 256 * c + 128 * hc + 128],
                                    XN[:, fc(c, t0):fc(c, t0) + 256],
                                    start=(c == 0), stop=(c == NCH - 1))
                            return i
                        emit("tensor", fmm1, waits=[(fsem, wld_u)],
                             reads=(bufslot,), writes=(hslot,), inc="pe_mm1")
                        mm1c = counts["pe_mm1"]
                        def fp1(eng, e=e, gh=gh, pcol=pcol):
                            return eng.scalar_tensor_tensor(
                                ASCR[:, 256 * (gh % 2):256 * (gh % 2) + 256],
                                PSA[:, pcol:pcol + 256], 0.0,
                                WBCS[:, 256 * e:256 * e + 256],
                                ALU.max, ALU.mult)
                        emit("vector", fp1,
                             waits=[W("pe_mm1", mm1c), W("sc_wbc")],
                             reads=(hslot, "wbcs%d" % e),
                             writes=("ascr%d" % (gh % 2),), inc="v_p1")
                        p1c = counts["v_p1"]
                        def fp2(eng, gh=gh):
                            a = ASCR[:, 256 * (gh % 2):256 * (gh % 2) + 256]
                            return eng.tensor_tensor(
                                HBF[:, 256 * gh:256 * gh + 256], a, a, ALU.mult)
                        emit("gpsimd", fp2, waits=[W("v_p1", p1c)],
                             reads=("ascr%d" % (gh % 2),),
                             writes=("hbf%d" % gh,), inc="po_p2")
                        p2cnt[hc] = counts["po_p2"]
                    def emit_mm2(hc, p2cnt=p2cnt, u=u, ub=ub, bufslot=bufslot):
                        gh = (2 * u + hc) % 4
                        def fmm2(eng, ub=ub, hc=hc, u=u, gh=gh):
                            i = None
                            for j in range(2):
                                yp = PSB if j == 0 else PSC
                                for uu in range(2):
                                    i = eng.matmul(
                                        yp[:, 512 * uu:512 * uu + 512],
                                        HBF[:, 256 * gh + 128 * j:
                                            256 * gh + 128 * j + 128],
                                        BW[:, ub + 2048 + 1024 * hc + 512 * uu:
                                           ub + 2048 + 1024 * hc + 512 * uu + 512],
                                        start=(u == 0 and hc == 0),
                                        stop=(u == NU - 1 and hc == 1))
                            return i
                        m2w = [W("po_p2", p2cnt[hc])]
                        if u == 0 and hc == 0:
                            m2w.append(("sc_g", counts["sc_g"]))
                        emit("tensor", fmm2, waits=m2w,
                             reads=("hbf%d" % ((2 * u + hc) % 4), bufslot),
                             writes=("psbc",) if (u == 0 and hc == 0) else (),
                             inc="pe_mm2")
                    for hc in range(2):
                        emit_mm1(hc)
                        pend_mm2.append(emit_mm2)
                        pend_hc.append(hc)
                        if len(pend_mm2) > 2:
                            fn_u = pend_mm2.pop(0)
                            fn_u(pend_hc.pop(0))
                while pend_mm2:
                    fn_u = pend_mm2.pop(0)
                    fn_u(pend_hc.pop(0))
                for j in range(2):
                    T = 2 * ps + j
                    yp = PSB if j == 0 else PSC
                    emit("sync", lambda eng, T=T: eng.dma_start(
                        XRL[:], dx[128 * T:128 * T + 128, :]),
                        waits=[W("v_ob")], inc="d_xr", ninc=16)
                    def fg(eng, T=T, yp=yp):
                        return eng.activation(GSB[:], yp[:], AF.Silu, bias=0.0,
                                              scale=RCPD[:, T:T + 1])
                    emit("scalar", fg,
                         waits=[W("pe_mm2"), W("v_route", 16), W("d_out")],
                         reads=("psbc",), writes=("gsb",), inc="sc_g")
                    def fstp(eng, T=T):
                        i = None
                        for c in range(NCH):
                            i = eng.transpose(PST[:, 128 * c:128 * c + 128],
                                              BB[:, fc(c, 128 * T):fc(c, 128 * T)
                                                 + 128], IDENT[:])
                        return i
                    emit("tensor", fstp, waits=[W("v_s")], writes=("pst",),
                         inc="pe_stp")
                    def foa(eng):
                        return eng.tensor_tensor(OUTA[:], PST[:], GSB[:], ALU.mult)
                    emit("vector", foa, waits=[W("pe_stp"), W("sc_g")],
                         reads=("pst", "gsb"), writes=("outa",), inc="v_oa")
                    def fob(eng):
                        return eng.tensor_tensor(OUTB[:], OUTA[:], XRL[:], ALU.add)
                    emit("vector", fob, waits=[W("d_xr"), W("d_out")],
                         reads=("outa", "xrl"), writes=("outb",), inc="v_ob")
                    emit("sync", lambda eng, T=T: eng.dma_start(
                        dout[128 * T:128 * T + 128, :], OUTB[:]),
                        waits=[W("v_ob")], reads=("outb",), inc="d_out", ninc=16)

        tapsrc = dict(xn=XN, a1=ATILE, qg=BA, kg0=K0, kg1=K1, vg=VA, gout=BB,
                      h1=BA, ql=BB, lout=BC, s=BB, szm=SZM, mkf=MKF, rcpd=RCPD,
                      qro=QRO, qn=QN, smallt=SMALLT)
        if tapt:
            snap = [(s, c) for s, c in counts.items() if c > 0]
            for name, ap in tapt.items():
                src = tapsrc[name]
                def ftap(eng, ap=ap, src=src):
                    return eng.dma_start(
                        ap, src[0:ap.shape[0], 0:ap.shape[1]])
                emit("sync", ftap, waits=snap, inc="d_out", ninc=16)

        final_out = counts["d_out"]

        semnames = sorted({inc for (_, _, _, inc, _) in prog if inc}
                          | {s for (_, _, ws, _, _) in prog for (s, _) in ws})
        sems = {n: en(nc.semaphore(n)) for n in semnames}
        block = en(nc.Block())

        def runner(ename):
            def body(eng):
                for (engine, fn, waits, inc, ninc) in prog:
                    if engine != ename:
                        continue
                    for (s, v) in waits:
                        eng.wait_ge(sems[s], v)
                    last = fn(eng)
                    if inc is not None:
                        assert last is not None, (engine, inc)
                        last.then_inc(sems[inc], ninc)
                if ename == "sync" and final_out > 0:
                    eng.wait_ge(sems["d_out"], final_out)
            return body

        block.sync(runner("sync"))
        block.tensor(runner("tensor"))
        block.vector(runner("vector"))
        block.scalar(runner("scalar"))
        block.gpsimd(runner("gpsimd"))

    return nc, tapt


def _perm_eo(w, head=64):
    ncols = w.shape[-1]
    idx = np.arange(ncols).reshape(-1, head)
    out = np.concatenate([idx[:, 0::2].reshape(-1), idx[:, 1::2].reshape(-1)])
    return np.ascontiguousarray(w[..., out])


def qhead_perm():
    perm = np.zeros(1024, dtype=np.int64)
    for c in range(8):
        a = (c % 4) + 8 * (c // 4)
        for d in range(64):
            perm[128 * c + d] = 64 * a + d
            perm[128 * c + 64 + d] = 64 * (a + 4) + d
    return perm


def _perm_wo(w):
    return np.ascontiguousarray(np.asarray(w)[qhead_perm(), :])


def prep_inputs(inputs):
    w = {
        "n1w": np.ascontiguousarray(np.asarray(inputs["L0_attn_norm1_w"])
                                    .reshape(D, 1)),
        "n2w": np.ascontiguousarray(np.asarray(inputs["L0_attn_norm2_w"])
                                    .reshape(D, 1)),
        "gwq": _perm_eo(np.asarray(inputs["L0_attn_g_wq"])),
        "gkvd": np.ascontiguousarray(inputs["L0_attn_g_kv_down"]),
        "gkup": _perm_eo(np.asarray(inputs["L0_attn_g_k_up"])),
        "gvup": np.ascontiguousarray(inputs["L0_attn_g_v_up"]),
        "gwo": _perm_wo(inputs["L0_attn_g_wo"]),
        "lwq": _perm_eo(np.asarray(inputs["L0_attn_l_wq"])),
        "lwk": _perm_eo(np.asarray(inputs["L0_attn_l_wk"])),
        "lwv": np.ascontiguousarray(inputs["L0_attn_l_wv"]),
        "lwo": _perm_wo(inputs["L0_attn_l_wo"]),
        "rwT": np.ascontiguousarray(np.asarray(inputs["L0_ffn_router_w"]).T),
        "w1": np.ascontiguousarray(inputs["L0_ffn_w1"]),
        "w2": np.ascontiguousarray(inputs["L0_ffn_w2"]),
    }
    w = {k: np.asarray(v, dtype=np.float32) for k, v in w.items()}
    xs = np.asarray(inputs["L0_attn_hidden"], dtype=np.float32)
    return [dict(w, x=np.ascontiguousarray(xs[b])) for b in range(8)]


def kernel(**inputs):
    in_maps = prep_inputs(inputs)
    nc, _ = build_nc()
    res = run_bass_kernel_spmd(nc, in_maps, core_ids=list(range(8)))
    return np.stack([r["out"] for r in res.results], axis=0)
